# version 14
# baseline (speedup 1.0000x reference)
import numpy as np
import concourse.bass as bass
import concourse.mybir as mybir
from concourse.bass_utils import run_bass_kernel_spmd
from contextlib import ExitStack

F32 = mybir.dt.float32
BF16 = mybir.dt.bfloat16
AF = mybir.ActivationFunctionType
ALU = mybir.AluOpType
AX = mybir.AxisListType

T = 4096
DM = 1024
DI = 2048
NST = 16
RK = 64
NE = 8
DH = 4096
NMC = DM // 128
NDC = DI // 128
EPS = 1e-5
CAP = 5120
TC = 256
NTC = T // TC
NTB = T // 128

ENGS = ("v", "s", "t", "g", "y")


DMA_SLOTS = {"y": 10, "g": 8, "s": 2, "v": 2}


class Emitter:

    def __init__(self):
        self.ops = {e: [] for e in ENGS}
        self.cnt = {e: 0 for e in ENGS}
        self.slotcnt = {(e, i): 0 for e, n in DMA_SLOTS.items() for i in range(n)}
        self.slot_rr = {e: 0 for e in DMA_SLOTS}
        self.pending = {e: {} for e in ENGS}
        self.waited = {e: {} for e in ENGS}

    def _prune(self, eng, waits):
        w = self.waited[eng]
        out = {}
        for k, v in waits.items():
            if k == ("e", eng):
                pass
            if v > w.get(k, 0):
                out[k] = v
                w[k] = v
        return out

    def barrier(self):
        toks = [self.marker(e) for e in ENGS if self.cnt[e] > 0] + self.dmarker("all")
        w = self._waits(toks)
        for e in ENGS:
            for k, v in w.items():
                self.pending[e][k] = max(self.pending[e].get(k, 0), v)

    def _take_pending(self, eng, waits):
        for k, v in self.pending[eng].items():
            waits[k] = max(waits.get(k, 0), v)
        self.pending[eng] = {}
        return waits

    @classmethod
    def _flat(cls, deps):
        for tok in deps:
            if tok is None:
                continue
            if isinstance(tok, list):
                yield from cls._flat(tok)
            else:
                yield tok

    def _waits(self, deps, eng=None):
        waits = {}
        for tok in self._flat(deps):
            if tok[0] == "e":
                _, de, dc, dsmall = tok
                key = ("e", de)
            else:
                _, slot, dc = tok
                key = ("d", slot)
            waits[key] = max(waits.get(key, 0), dc)
        return waits

    def emit(self, eng, fn, deps=(), small=False):
        waits = self._prune(eng, self._take_pending(eng, self._waits(deps, eng)))
        self.cnt[eng] += 1

        def closure(b, waits=waits, fn=fn, eng=eng):
            for (kind, name), val in waits.items():
                b.wait_ge(self.sems[name] if kind == "e" else self.dsems[name], val)
            fn(b).then_inc(self.sems[eng], 1)
        self.ops[eng].append(closure)
        return ("e", eng, self.cnt[eng], small)

    def dma(self, eng, out, in_, deps=(), queue=None, slow=False, accum=False):
        si = self.slot_rr[eng]
        self.slot_rr[eng] = (si + 1) % DMA_SLOTS[eng]
        slot = (eng, si)
        prior = self.slotcnt[slot]
        self.slotcnt[slot] = prior + 16
        waits = self._prune(eng, self._take_pending(eng, self._waits(deps)))
        if prior and prior > self.waited[eng].get(("d", slot), 0):
            self.waited[eng][("d", slot)] = prior

        def closure(b, waits=waits, out=out, in_=in_, slow=slow, slot=slot,
                    prior=prior, accum=accum):
            for (kind, name), val in waits.items():
                b.wait_ge(self.sems[name] if kind == "e" else self.dsems[name], val)
            if prior:
                b.wait_ge(self.dsems[slot], prior)
            kw = dict(allow_slow_non_contiguous=slow)
            if accum:
                kw["accum_op"] = mybir.AluOpType.add
            b.dma_start(out=out, in_=in_, **kw).then_inc(self.dsems[slot], 16)
        self.ops[eng].append(closure)
        return ("d", slot, prior + 16), None

    def marker(self, eng):
        return ("e", eng, self.cnt[eng], True)

    def dmarker(self, queue):
        engs = {"ld": ("y", "s", "v"), "st": ("g",), "all": tuple(DMA_SLOTS)}[queue]
        return [("d", (e, i), self.slotcnt[(e, i)])
                for e in engs for i in range(DMA_SLOTS[e])
                if self.slotcnt[(e, i)] > 0]

    def finalize(self, block, sems, dsems):
        self.sems = sems
        self.dsems = dsems
        for eng, dec in (("v", block.vector), ("s", block.scalar),
                         ("t", block.tensor), ("g", block.gpsimd),
                         ("y", block.sync)):
            if not self.ops[eng]:
                continue

            def make(lst):
                def _(b):
                    for f in lst:
                        f(b)
                return _
            dec(make(self.ops[eng]))


def build_nc(debug=False, stop_after=None, fake_cc=False):
    nc = bass.Bass("TRN2", target_bir_lowering=False)

    def din(name, shape):
        return nc.dram_tensor(name, list(shape), F32, kind="ExternalInput").ap()

    I = {}
    I["x"] = din("x", (T, DM))
    I["norm_g"] = din("norm_g", (DM,))
    I["norm_b"] = din("norm_b", (DM,))
    for p in ("f", "b"):
        for nm, sh in (("Win", (2 * DI, DM)), ("convw", (DI, 4)), ("convb", (DI,)),
                       ("Wx", (RK + 2 * NST, DI)), ("Wdt", (DI, RK)), ("bdt", (DI,)),
                       ("Alog", (DI, NST)), ("D", (DI,)), ("Wout", (DM, DI))):
            I[f"{p}_{nm}"] = din(f"{p}_{nm}", sh)
    I["router_w"] = din("router_w", (DM, NE))
    I["router_b"] = din("router_b", (NE,))
    I["w1"] = din("w1", (NE, DM, DH))
    I["b1"] = din("b1", (NE, DH))
    I["w2"] = din("w2", (NE, DH, DM))
    I["b2"] = din("b2", (NE, DM))
    I["prefmask"] = din("prefmask", (NE, 1))

    out_h = nc.dram_tensor("out", [T, DM], F32, kind="ExternalOutput").ap()

    scratch_kind = "ExternalOutput" if debug else "Internal"

    def scr(name, shape, dt=F32):
        return nc.dram_tensor(name, list(shape), dt, kind=scratch_kind).ap()

    xT_h = scr("xT_h", (DM, T))
    xn_h = scr("xn_h", (DM, T)) if debug else None
    u_h = {p: scr(f"u_{p}_h", (DI, T)) for p in ("f", "b")}
    sz_h = {p: scr(f"sz_{p}_h", (DI, T)) for p in ("f", "b")}
    xdbl_h = {p: scr(f"xdbl_{p}_h", (96, T)) for p in ("f", "b")} if debug else None
    yp_h = {p: (scr(f"yp_{p}_h", (DI, T)) if debug else None) for p in ("f", "b")}
    x2pre_h = scr("x2pre_h", (DM, T))
    x2_h = scr("x2_h", (DM, T))
    x2bf_h = nc.dram_tensor("x2bf_h", [DM, T], BF16, kind="Internal").ap()
    ymoe_h = scr("ymoe_h", (DM, T))
    rt_h = scr("rt_h", (128, NTB * NE * 3)) if debug else None
    cc_in = nc.dram_tensor("cc_in", [1, NE], F32, kind="Internal").ap()
    cc_out = nc.dram_tensor("cc_out", [1, NE * 8], F32, kind="Internal",
                            addr_space=("Local" if fake_cc else "Shared")).ap()

    ident = nc.inline_tensor(np.eye(128, dtype=np.float32), "ident").ap()
    ones_r = nc.inline_tensor(np.ones((1, 128), dtype=np.float32), "ones_r").ap()
    tri = nc.inline_tensor(
        (np.arange(128)[:, None] <= np.arange(128)[None, :]).astype(np.float32), "tri").ap()
    import ml_dtypes
    _sel = np.zeros((96, 32 * 128), dtype=ml_dtypes.bfloat16)
    for _n in range(NST):
        _sel[64 + _n, (2 * _n) * 128:(2 * _n + 1) * 128] = 1.0
        _sel[80 + _n, (2 * _n + 1) * 128:(2 * _n + 2) * 128] = 1.0
    selbc = nc.inline_tensor(_sel, "selbc").ap()
    _sel8 = np.zeros((8, 8 * 128), dtype=ml_dtypes.bfloat16)
    for _e in range(8):
        _sel8[_e, _e * 128:(_e + 1) * 128] = 1.0
    sel8 = nc.inline_tensor(_sel8, "sel8").ap()

    E = Emitter()

    with ExitStack() as ctx:
        def sb_in(pool, name, shape, dt=F32):
            return pool.enter_context(nc.sbuf_tensor(name, shape, dt))

        def sb(name, shape, dt=F32):
            return ctx.enter_context(nc.sbuf_tensor(name, shape, dt))

        def psum(name, shape, dt=F32):
            return ctx.enter_context(nc.psum_tensor(name, shape, dt))

        block = ctx.enter_context(nc.Block())
        sems = {e: ctx.enter_context(nc.semaphore(f"ch_{e}")) for e in ENGS}
        dsems = {(e, i): ctx.enter_context(nc.semaphore(f"dma_{e}{i}"))
                 for e, n in DMA_SLOTS.items() for i in range(n)}

        identt = sb("identt", [128, 128])
        ones_rt = sb("ones_rt", [1, 128])
        trit = sb("trit", [128, 128])
        onesb = sb("onesb", [128, 1], BF16)
        ones_rbf = sb("ones_rbf", [1, 128], BF16)
        g_pp = sb("g_pp", [128, NMC])
        b_pp = sb("b_pp", [128, NMC])
        gates_sb = sb("gates_sb", [128, NTB * NE])
        mask_sb = sb("mask_sb", [128, NTB * NE])
        w_sb = sb("w_sb", [128, NTB * NE])
        pool_p2 = ExitStack()
        wdtT = {p: sb_in(pool_p2, f"wdtT{p}", [64, NDC * 128], BF16) for p in ("f", "b")}
        A_sb = {p: sb_in(pool_p2, f"A_sb{p}", [128, NDC * NST]) for p in ("f", "b")}
        D_sb = {p: sb_in(pool_p2, f"D_sb{p}", [128, NDC]) for p in ("f", "b")}
        bdt_sb = {p: sb_in(pool_p2, f"bdt_sb{p}", [128, NDC]) for p in ("f", "b")}
        xdbl_sb = {p: sb_in(pool_p2, f"xdbl_{p}", [96, T]) for p in ("f", "b")}
        rw_sb = sb_in(pool_p2, "rw_sb", [128, NMC * NE])
        rb_bc = sb_in(pool_p2, "rb_bc", [128, NE])
        sel_sb = sb_in(pool_p2, "sel_sb", [96, 32 * 128], BF16)
        pool_xn = ExitStack()
        xn_bf = sb_in(pool_xn, "xn_bf", [128, NMC * T], BF16)

        pool1 = ExitStack()
        xtile = sb_in(pool1, "xtile", [128, DM])
        xm = [sb_in(pool1, f"xm{j}", [128, 256]) for j in range(NMC)]
        sq_bf = sb_in(pool1, "sq_bf", [128, 256], BF16)
        x_bf = sb_in(pool1, "x_bf", [128, 256], BF16)
        st_mean = sb_in(pool1, "st_mean", [1, 256]); st_ex2 = sb_in(pool1, "st_ex2", [1, 256])
        st_tmp = sb_in(pool1, "st_tmp", [1, 256]); st_rstd = sb_in(pool1, "st_rstd", [1, 256])
        nt1 = sb_in(pool1, "nt1", [128, 256])

        P = [psum(f"P{i}", [128, 512]) for i in range(8)]

        d_id, _ = E.dma("y", identt[:], ident)
        d_or, _ = E.dma("y", ones_rt[:], ones_r)
        d_tri, _ = E.dma("y", trit[:], tri)
        d_g, _ = E.dma("y", g_pp[:], I["norm_g"].rearrange("(c p) -> p c", p=128), slow=True)
        d_b, _ = E.dma("y", b_pp[:], I["norm_b"].rearrange("(c p) -> p c", p=128), slow=True)
        tok_ones = E.emit("v", lambda b: b.memset(onesb[:], 1.0), small=True)
        tok_onesr = E.emit("v", lambda b: b.memset(ones_rbf[:], 1.0), small=True)

        prev_ev = [None] * NMC
        prev_mm_last = None
        for tb in range(NTB):
            d_in, _ = E.dma("y", xtile[:], I["x"][tb * 128:(tb + 1) * 128, :],
                            deps=[prev_mm_last])
            mm_last = None
            for j in range(NMC):
                mm = E.emit("t", lambda b, j=j: b.matmul(
                    P[j][:, :128], xtile[:, j * 128:(j + 1) * 128], identt[:],
                    start=True, stop=True),
                    deps=[d_in, d_id, prev_ev[j]], small=True)
                ev = E.emit("v", lambda b, j=j: b.tensor_copy(xm[j][:, :128], P[j][:, :128]),
                            deps=[mm], small=True)
                prev_ev[j] = ev
                mm_last = mm
                E.dma("g", xT_h[j * 128:(j + 1) * 128, tb * 128:(tb + 1) * 128],
                      xm[j][:, :128], deps=[ev], queue="st")
            prev_mm_last = E.marker("t")
            prev_ev = [E.dmarker("st")] * NMC if False else prev_ev
            prev_ev = [E.dmarker("st") for _ in range(NMC)]
        xT_done = E.dmarker("st")

        prev_norm = None
        prev_stats = None
        for tc in range(T // 256):
            lds = []
            for j in range(NMC):
                d, _ = E.dma("y", xm[j][:, :],
                             xT_h[j * 128:(j + 1) * 128, tc * 256:(tc + 1) * 256],
                             deps=[xT_done, prev_norm])
                lds.append(d)
            alld = E.dmarker("ld")
            mm_toks = []
            sq_prev = cp_prev = None
            for j in range(NMC):
                sqt = E.emit("s", lambda b, j=j: b.activation(
                    out=sq_bf[:], in_=xm[j][:], func=AF.Square),
                    deps=[alld, mm_toks[-1] if mm_toks else None])
                cpt = E.emit("v", lambda b, j=j: b.tensor_copy(x_bf[:], xm[j][:]),
                             deps=[alld, mm_toks[-2] if len(mm_toks) >= 2 else None])
                m1 = E.emit("t", lambda b, j=j: b.matmul(
                    P[2][0:1, :256], onesb[:], x_bf[:],
                    start=(j == 0), stop=(j == NMC - 1)),
                    deps=[cpt, tok_ones, prev_stats], small=True)
                m2 = E.emit("t", lambda b, j=j: b.matmul(
                    P[3][0:1, :256], onesb[:], sq_bf[:],
                    start=(j == 0), stop=(j == NMC - 1)),
                    deps=[sqt, prev_stats], small=True)
                mm_toks += [m1, m2]
            t_mean = E.emit("v", lambda b: b.tensor_scalar(
                out=st_mean[:], in0=P[2][0:1, :256], scalar1=1.0 / DM, scalar2=None,
                op0=ALU.mult), deps=mm_toks, small=True)
            t_ex2 = E.emit("v", lambda b: b.tensor_scalar(
                out=st_ex2[:], in0=P[3][0:1, :256], scalar1=1.0 / DM, scalar2=None,
                op0=ALU.mult), deps=mm_toks, small=True)
            prev_stats = t_ex2
            t_m2 = E.emit("v", lambda b: b.tensor_tensor(
                out=st_tmp[:], in0=st_mean[:], in1=st_mean[:], op=ALU.mult),
                deps=[t_mean], small=True)
            t_var = E.emit("v", lambda b: b.tensor_tensor(
                out=st_ex2[:], in0=st_ex2[:], in1=st_tmp[:], op=ALU.subtract),
                deps=[t_ex2, t_m2], small=True)
            t_ve = E.emit("v", lambda b: b.tensor_scalar(
                out=st_ex2[:], in0=st_ex2[:], scalar1=float(EPS), scalar2=None,
                op0=ALU.add), deps=[t_var], small=True)
            t_sd = E.emit("s", lambda b: b.activation(
                out=st_tmp[:], in_=st_ex2[:], func=AF.Sqrt),
                deps=[t_ve], small=True)
            t_rs = E.emit("v", lambda b: b.reciprocal(st_rstd[:], st_tmp[:]),
                          deps=[t_sd], small=True)
            bc_m = E.emit("t", lambda b: b.matmul(
                P[4][:, :256], ones_rt[:], st_mean[:], start=True, stop=True),
                deps=[t_mean, t_m2, d_or, prev_norm], small=True)
            bc_r = E.emit("t", lambda b: b.matmul(
                P[5][:, :256], ones_rt[:], st_rstd[:], start=True, stop=True),
                deps=[t_rs, prev_norm], small=True)
            norm_toks = []
            sprev = None
            for j in range(NMC):
                s1 = E.emit("v", lambda b, j=j: b.tensor_tensor(
                    out=nt1[:], in0=xm[j][:], in1=P[4][:, :256], op=ALU.subtract),
                    deps=[bc_m, sprev])
                s2 = E.emit("v", lambda b, j=j: b.tensor_tensor(
                    out=nt1[:], in0=nt1[:], in1=P[5][:, :256], op=ALU.mult),
                    deps=[bc_r, s1])
                s3 = E.emit("v", lambda b, j=j, tc=tc: b.tensor_scalar(
                    out=xn_bf[:, j * T + tc * 256:j * T + (tc + 1) * 256],
                    in0=nt1[:], scalar1=g_pp[:, j:j + 1], scalar2=b_pp[:, j:j + 1],
                    op0=ALU.mult, op1=ALU.add),
                    deps=[d_g, d_b, s2])
                sprev = s3
                norm_toks.append(s3)
            prev_norm = norm_toks[-1]
            if debug:
                for j in range(NMC):
                    cp = E.emit("v", lambda b, j=j, tc=tc: b.tensor_copy(
                        xm[j][:], xn_bf[:, j * T + tc * 256:j * T + (tc + 1) * 256]),
                        deps=[norm_toks[j]])
                    _, et = E.dma("g", xn_h[j * 128:(j + 1) * 128, tc * 256:(tc + 1) * 256],
                                  xm[j][:], deps=[cp], queue="st")
                prev_norm = E.dmarker("st")

        pool1.close()
        E.barrier()
        SKIP_AB = stop_after == "p1"
        xn_ready = E.marker("v")
        pool_ab = ExitStack()
        wrow = sb_in(pool_ab, "wrow", [128, DM])
        wrow2 = sb_in(pool_ab, "wrow2", [128, DM])
        winuT = sb_in(pool_ab, "winuT", [128, NMC * 128], BF16)
        winzT = sb_in(pool_ab, "winzT", [128, NMC * 128], BF16)
        wx_sb = sb_in(pool_ab, "wx_sb", [96, DI])
        wxT = sb_in(pool_ab, "wxT", [128, NDC * 96], BF16)
        wdt_sb = sb_in(pool_ab, "wdt_sb", [128, RK])
        convb_sb = sb_in(pool_ab, "convb_sb", [128, NDC])
        convw_sb = sb_in(pool_ab, "convw_sb", [128, NDC * 4])
        u0_full = sb_in(pool_ab, "u0_full", [128, T])
        uc_full = sb_in(pool_ab, "uc_full", [128, T])
        ubf_d = sb_in(pool_ab, "ubf_d", [128, T], BF16)
        sz_st = sb_in(pool_ab, "sz_st", [128, T])
        u_st = u0_full

        dir_dep = None
        for p in ([] if SKIP_AB else (["f"] if stop_after == "fwd" else ["f", "b"])):
            E.barrier()
            d_wx, _ = E.dma("y", wx_sb[:], I[f"{p}_Wx"])
            d_ab, _ = E.dma("y", A_sb[p][:].rearrange("p (c n) -> p c n", n=NST),
                            I[f"{p}_Alog"].rearrange("(c p) n -> p c n", p=128), slow=True)
            d_D, _ = E.dma("y", D_sb[p][:], I[f"{p}_D"].rearrange("(c p) -> p c", p=128), slow=True)
            d_cb, _ = E.dma("y", convb_sb[:], I[f"{p}_convb"].rearrange("(c p) -> p c", p=128), slow=True)
            d_bd, _ = E.dma("y", bdt_sb[p][:], I[f"{p}_bdt"].rearrange("(c p) -> p c", p=128), slow=True)
            d_cw, _ = E.dma("y", convw_sb[:].rearrange("p (c k) -> p c k", k=4),
                            I[f"{p}_convw"].rearrange("(c p) k -> p c k", p=128), slow=True)
            tA1 = E.emit("s", lambda b, p=p: b.activation(out=A_sb[p][:], in_=A_sb[p][:], func=AF.Exp),
                         deps=[d_ab])
            tA2 = E.emit("v", lambda b, p=p: b.tensor_scalar(
                out=A_sb[p][:], in0=A_sb[p][:], scalar1=-1.0, scalar2=None, op0=ALU.mult),
                deps=[tA1])
            wx_toks = []
            for d in range(NDC):
                mm = E.emit("t", lambda b, d=d: b.matmul(
                    P[6][0:128, :96], wx_sb[:, d * 128:(d + 1) * 128],
                    identt[0:96, 0:96], start=True, stop=True),
                    deps=[d_wx, d_id, wx_toks[-1] if wx_toks else None], small=True)
                ev = E.emit("v", lambda b, d=d: b.tensor_copy(
                    wxT[:, d * 96:(d + 1) * 96], P[6][0:128, :96]), deps=[mm], small=True)
                wx_toks.append(ev)
            wdt_toks = []
            for d in range(NDC):
                d_w, _ = E.dma("y", wdt_sb[:], I[f"{p}_Wdt"][d * 128:(d + 1) * 128, :],
                               deps=[wdt_toks[-1] if wdt_toks else None])
                mm = E.emit("t", lambda b, d=d: b.matmul(
                    P[6][0:64, :128], wdt_sb[:], identt[:],
                    start=True, stop=True),
                    deps=[d_w, wdt_toks[-1] if wdt_toks else wx_toks[-1]], small=True)
                ev = E.emit("v", lambda b, d=d, p=p: b.tensor_copy(
                    wdtT[p][:, d * 128:(d + 1) * 128], P[6][0:64, :128]), deps=[mm], small=True)
                wdt_toks.append(ev)
            wprep = E.marker("v")

            prev_iter = {"c": [dir_dep, wdt_toks[-1], tA2], "x": None, "u": None,
                         "su": None, "cbf": None, "cvt": None, "e1": None, "xa": None,
                         "dsz": None, "du_st": None}
            for d in range(NDC):
                du, _ = E.dma("y", wrow[:], I[f"{p}_Win"][d * 128:(d + 1) * 128, :],
                              deps=[prev_iter["c"]])
                dz, _ = E.dma("y", wrow2[:], I[f"{p}_Win"][DI + d * 128:DI + (d + 1) * 128, :],
                              deps=[prev_iter["c"]])
                tp_toks = []
                for j in range(NMC):
                    mm = E.emit("t", lambda b, j=j: b.matmul(
                        P[6][:, :128], wrow[:, j * 128:(j + 1) * 128], identt[:],
                        start=True, stop=True),
                        deps=[du, tp_toks[-1] if tp_toks else None, prev_iter["c"]], small=True)
                    ev = E.emit("v", lambda b, j=j: b.tensor_copy(
                        winuT[:, j * 128:(j + 1) * 128], P[6][:, :128]), deps=[mm], small=True)
                    tp_toks.append(ev)
                    mm2 = E.emit("t", lambda b, j=j: b.matmul(
                        P[6][:, :128], wrow2[:, j * 128:(j + 1) * 128], identt[:],
                        start=True, stop=True), deps=[dz, ev], small=True)
                    ev2 = E.emit("v", lambda b, j=j: b.tensor_copy(
                        winzT[:, j * 128:(j + 1) * 128], P[6][:, :128]), deps=[mm2], small=True)
                    tp_toks.append(ev2)
                tpz = tp_toks[-1]
                ev_toks = []
                for tq in range(T // 512):
                    mmu = mmz = None
                    for j in range(NMC):
                        mmu = E.emit("t", lambda b, j=j, tq=tq: b.matmul(
                            P[0][:, :512], winuT[:, j * 128:(j + 1) * 128],
                            xn_bf[:, j * T + tq * 512:j * T + (tq + 1) * 512],
                            start=(j == 0), stop=(j == NMC - 1)),
                            deps=[tpz, xn_ready, ev_toks[-2] if len(ev_toks) >= 2 else None],
                            small=True)
                        mmz = E.emit("t", lambda b, j=j, tq=tq: b.matmul(
                            P[1][:, :512], winzT[:, j * 128:(j + 1) * 128],
                            xn_bf[:, j * T + tq * 512:j * T + (tq + 1) * 512],
                            start=(j == 0), stop=(j == NMC - 1)),
                            deps=[tpz, xn_ready, ev_toks[-1] if ev_toks else None],
                            small=True)
                    e1 = E.emit("s", lambda b, tq=tq: b.copy(
                        u0_full[:, tq * 512:(tq + 1) * 512], P[0][:, :512]),
                        deps=[mmu, prev_iter["cvt"], prev_iter["du_st"],
                              prev_iter["cbf"], dir_dep])
                    e2 = E.emit("s", lambda b, tq=tq: b.activation(
                        out=sz_st[:, tq * 512:(tq + 1) * 512], in_=P[1][:, :512],
                        func=AF.Silu), deps=[mmz, prev_iter["dsz"]])
                    ev_toks += [e1, e2]
                d_sz, _ = E.dma("g", sz_h[p][d * 128:(d + 1) * 128, :], sz_st[:],
                                deps=[ev_toks[-1]], queue="st")
                u0r = E.marker("s")
                cv0 = E.emit("v", lambda b, d=d: b.tensor_scalar(
                    out=uc_full[:], in0=u0_full[:],
                    scalar1=convw_sb[:, d * 4 + 3:d * 4 + 4], scalar2=None, op0=ALU.mult),
                    deps=[u0r, d_cw, prev_iter["u"]])
                cvt = cv0
                for k in (2, 1, 0):
                    s = 3 - k
                    if p == "f":
                        def cfn(b, k=k, s=s, d=d):
                            return b.scalar_tensor_tensor(
                                out=uc_full[:, s:], in0=u0_full[:, :T - s],
                                scalar=convw_sb[:, d * 4 + k:d * 4 + k + 1],
                                in1=uc_full[:, s:], op0=ALU.mult, op1=ALU.add)
                    else:
                        def cfn(b, k=k, s=s, d=d):
                            return b.scalar_tensor_tensor(
                                out=uc_full[:, :T - s], in0=u0_full[:, s:],
                                scalar=convw_sb[:, d * 4 + k:d * 4 + k + 1],
                                in1=uc_full[:, :T - s], op0=ALU.mult, op1=ALU.add)
                    cvt = E.emit("v", cfn, deps=[cvt])
                su = E.emit("s", lambda b, d=d: b.activation(
                    out=u_st[:], in_=uc_full[:], func=AF.Silu,
                    bias=convb_sb[:, d:d + 1]),
                    deps=[cvt, d_cb, prev_iter["du_st"], prev_iter["cbf"]])
                d_u, _ = E.dma("g", u_h[p][d * 128:(d + 1) * 128, :], u_st[:],
                               deps=[su], queue="st")
                cbf = E.emit("v", lambda b: b.tensor_copy(ubf_d[:], u_st[:]),
                             deps=[su, prev_iter["xa"]])
                xa_toks = []
                for tq in range(T // 512):
                    mm = E.emit("t", lambda b, d=d, tq=tq: b.matmul(
                        P[7][0:96, :512], wxT[:, d * 96:(d + 1) * 96],
                        ubf_d[:, tq * 512:(tq + 1) * 512], start=True, stop=True),
                        deps=[cbf, wprep, xa_toks[-1] if xa_toks else None], small=True)
                    if d == 0:
                        add = E.emit("v", lambda b, tq=tq, p=p: b.tensor_copy(
                            xdbl_sb[p][:, tq * 512:(tq + 1) * 512], P[7][0:96, :512]),
                            deps=[mm])
                    else:
                        add = E.emit("v", lambda b, tq=tq, p=p: b.tensor_tensor(
                            out=xdbl_sb[p][:, tq * 512:(tq + 1) * 512],
                            in0=xdbl_sb[p][:, tq * 512:(tq + 1) * 512],
                            in1=P[7][0:96, :512], op=ALU.add), deps=[mm])
                    xa_toks.append(add)
                prev_iter = {"c": xa_toks[-1], "x": None, "u": E.dmarker("st"),
                             "su": su, "cbf": cbf, "cvt": cvt, "e1": None,
                             "xa": E.marker("t"), "dsz": d_sz, "du_st": d_u}
            if debug:
                xa_fin = [E.marker("v")] + E.dmarker("st")
                for tq in range(T // 512):
                    cpd = E.emit("v", lambda b, tq=tq, p=p: b.tensor_copy(
                        u0_full[0:96, tq * 512:(tq + 1) * 512],
                        xdbl_sb[p][:, tq * 512:(tq + 1) * 512]), deps=[xa_fin])
                    E.dma("g", xdbl_h[p][0:96, tq * 512:(tq + 1) * 512],
                          u0_full[0:96, tq * 512:(tq + 1) * 512], deps=[cpd], queue="st")
            dir_dep = E.dmarker("st")

        pool_ab.close()
        pool_xn.close()
        E.barrier()

        pool_c = ExitStack()
        woutT = sb_in(pool_c, "woutT", [128, NDC * DM], BF16)
        dt_c = sb_in(pool_c, "dt_c", [128, NDC * TC])
        dtu_c = sb_in(pool_c, "dtu_c", [128, NDC * TC])
        u_c = sb_in(pool_c, "u_c", [128, NDC * TC])
        _x2alias = True
        sz_c = sb_in(pool_c, "sz_c", [128, NDC * TC])
        Bs2 = [sb_in(pool_c, f"Bs{i}", [128, TC]) for i in range(2)]
        Cs2 = [sb_in(pool_c, f"Cs{i}", [128, TC]) for i in range(2)]
        a_t2 = [sb_in(pool_c, f"a_t{i}", [128, TC]) for i in range(2)]
        b_t2 = [sb_in(pool_c, f"b_t{i}", [128, TC]) for i in range(2)]
        h_t2 = [sb_in(pool_c, f"h_t{i}", [128, TC]) for i in range(2)]
        prod2 = [sb_in(pool_c, f"prod{i}", [128, TC]) for i in range(2)]
        a_t = a_t2[0]; b_t = b_t2[0]; h_t = h_t2[0]; prod = prod2[0]
        y_acc = sb_in(pool_c, "y_acc", [128, NDC * TC])
        yp_bf = sb_in(pool_c, "yp_bf", [128, NDC * TC], BF16)
        carry = sb_in(pool_c, "carry", [128, NDC * NST])
        xdbl_bf = sb_in(pool_c, "xdbl_bf", [96, TC], BF16)
        x2_c = u_c
        xres = [sb_in(pool_c, f"xres{i}", [128, TC]) for i in range(2)]
        stg = xres
        stall = sb_in(pool_c, "stall", [128, NMC * TC])
        wrow_c = stall
        c_sqbf = sb_in(pool_c, "c_sqbf", [128, TC], BF16)
        c_xbf = sb_in(pool_c, "c_xbf", [128, TC], BF16)
        c_stm = sb_in(pool_c, "c_stm", [1, TC]); c_ste = sb_in(pool_c, "c_ste", [1, TC])
        c_stt = sb_in(pool_c, "c_stt", [1, TC]); c_str = sb_in(pool_c, "c_str", [1, TC])
        x2bf_c = yp_bf
        gt_ps = sb_in(pool_c, "gt_ps", [128, NE])
        m1_ps = sb_in(pool_c, "m1_ps", [128, 1]); m2_ps = sb_in(pool_c, "m2_ps", [128, 1])
        eq_ps = sb_in(pool_c, "eq_ps", [128, NE]); sum_ps = sb_in(pool_c, "sum_ps", [128, 1])

        d_rw, _ = E.dma("y", rw_sb[:].rearrange("p (c e) -> p c e", e=NE),
                        I["router_w"].rearrange("(c p) e -> p c e", p=128), slow=True)
        d_rb, _ = E.dma("y", rb_bc[:],
                        I["router_b"].unsqueeze(0).squeeze(0).partition_broadcast(128))
        d_sel, _ = E.dma("y", sel_sb[:], selbc)

        if stop_after in ("p1", "fwd", "ab"):
            dirs_c = []
        elif stop_after == "cfwd":
            dirs_c = ["f"]
        else:
            dirs_c = ["f", "b"]
        for p in dirs_c:
            E.barrier()
            wo_toks = []
            for mq in range(NMC):
                d_w, _ = E.dma("y", wrow_c[:], I[f"{p}_Wout"][mq * 128:(mq + 1) * 128, :],
                               deps=[wo_toks[-1] if wo_toks else None])
                for d in range(NDC):
                    mm = E.emit("t", lambda b, d=d: b.matmul(
                        P[1][:, :128], wrow_c[:, d * 128:(d + 1) * 128], identt[:],
                        start=True, stop=True),
                        deps=[d_w, wo_toks[-1] if wo_toks else None], small=True)
                    ev = E.emit("v", lambda b, d=d, mq=mq: b.tensor_copy(
                        woutT[:, d * DM + mq * 128:d * DM + (mq + 1) * 128],
                        P[1][:, :128]), deps=[mm], small=True)
                    wo_toks.append(ev)
            wo_done = wo_toks[-1]

            tc_order = range(NTC) if p == "f" else range(NTC - 1, -1, -1)
            first_tci = NTC - 1 if p == "b" else 0
            prev = {"uld": None, "dtmm": None, "soft": None, "dtu": None,
                    "yp": None, "stg": [None, None], "xr": [None, None],
                    "bc": [None, None], "x2": None, "rt": None, "wmm": None,
                    "spv": None, "sps": None}
            for tci in tc_order:
                t0 = tci * TC
                uld_toks = []
                for d in range(NDC):
                    du_, _ = E.dma("y", u_c[:, d * TC:(d + 1) * TC],
                                   u_h[p][d * 128:(d + 1) * 128, t0:t0 + TC],
                                   deps=[prev["yp"], prev["x2"], prev["rt"]])
                    ds_, _ = E.dma("y", sz_c[:, d * TC:(d + 1) * TC],
                                   sz_h[p][d * 128:(d + 1) * 128, t0:t0 + TC],
                                   deps=[prev["yp"]])
                    uld_toks += [du_, ds_]
                xc = E.emit("v", lambda b, t0=t0, p=p: b.tensor_copy(
                    xdbl_bf[:], xdbl_sb[p][:, t0:t0 + TC]), deps=[prev["dtmm"]])
                soft_toks = []
                for d in range(NDC):
                    mm = E.emit("t", lambda b, d=d, p=p: b.matmul(
                        P[0][:, :TC], wdtT[p][:, d * 128:(d + 1) * 128],
                        xdbl_bf[0:64, :], start=True, stop=True),
                        deps=[xc, prev["soft"], prev["spv"],
                              soft_toks[-1] if soft_toks else None],
                        small=True)
                    sp0 = E.emit("v", lambda b, d=d, p=p: b.tensor_scalar(
                        out=a_t[:], in0=P[0][:, :TC], scalar1=bdt_sb[p][:, d:d + 1],
                        scalar2=None, op0=ALU.add),
                        deps=[mm, prev["dtu"], prev["spv"]])
                    sp1 = E.emit("s", lambda b: b.activation(
                        out=b_t[:], in_=a_t[:], func=AF.Abs), deps=[sp0, prev["sps"]])
                    sp2 = E.emit("s", lambda b: b.activation(
                        out=b_t[:], in_=b_t[:], func=AF.Exp, scale=-1.0), deps=[sp1])
                    sp3 = E.emit("s", lambda b: b.activation(
                        out=b_t[:], in_=b_t[:], func=AF.Ln, bias=1.0), deps=[sp2])
                    sp4 = E.emit("s", lambda b: b.activation(
                        out=h_t[:], in_=a_t[:], func=AF.Relu), deps=[sp0])
                    so = E.emit("v", lambda b, d=d: b.tensor_tensor(
                        out=dt_c[:, d * TC:(d + 1) * TC], in0=b_t[:], in1=h_t[:],
                        op=ALU.add), deps=[sp3, sp4])
                    prev["spv"] = so
                    prev["sps"] = sp4
                    soft_toks.append(so)
                    tu = E.emit("v", lambda b, d=d: b.tensor_tensor(
                        out=dtu_c[:, d * TC:(d + 1) * TC],
                        in0=dt_c[:, d * TC:(d + 1) * TC],
                        in1=u_c[:, d * TC:(d + 1) * TC], op=ALU.mult),
                        deps=[so, uld_toks[2 * d]])
                prev["soft"] = soft_toks[-1]
                prev["dtmm"] = E.marker("t")
                dtu_done = [E.marker("v"), E.marker("s")]
                prev["dtu"] = dtu_done
                acc_last = None
                prevs = {}
                ac_by_d = {}
                pend = None
                kidx = 0
                for n in range(NST):
                    pb = n % 2
                    PB = P[4 + pb]
                    PC = P[6 + pb]
                    bc_dep = [E.marker("v"), E.marker("g")]
                    dB = E.emit("t", lambda b, n=n, PB=PB: b.matmul(
                        PB[:, :TC], sel_sb[:, (2 * n) * 128:(2 * n + 1) * 128],
                        xdbl_bf[:], start=True, stop=True),
                        deps=[xc, d_sel, bc_dep, prev["x2"], prev["rt"]],
                        small=True)
                    dC = E.emit("t", lambda b, n=n, PC=PC: b.matmul(
                        PC[:, :TC], sel_sb[:, (2 * n + 1) * 128:(2 * n + 2) * 128],
                        xdbl_bf[:], start=True, stop=True),
                        deps=[xc, d_sel, bc_dep, prev["x2"], prev["rt"]],
                        small=True)
                    for d in range(NDC):
                        ki = kidx % 2
                        ka = kidx % 2
                        PA = a_t2[ka]
                        bt = b_t2[ki]; ht = h_t2[ki]; pt = prod2[ki]
                        sa = E.emit("s", lambda b, d=d, n=n, p=p, PA=PA: b.activation(
                            out=PA[:, :TC], in_=dt_c[:, d * TC:(d + 1) * TC], func=AF.Exp,
                            scale=A_sb[p][:, d * NST + n:d * NST + n + 1]),
                            deps=[dtu_done, prevs.get(("sca", ka))])
                        bt_ = E.emit("v", lambda b, d=d, PB=PB, bt=bt: b.tensor_tensor(
                            out=bt[:], in0=dtu_c[:, d * TC:(d + 1) * TC],
                            in1=PB[:, :TC], op=ALU.mult),
                            deps=[dB, dtu_done, prevs.get(("sc", ki))])
                        if pend is not None:
                            E_n, E_d, E_ki, E_sc, E_PC, E_dC = pend
                            if E_n == 0:
                                ac = E.emit("v", lambda b, E_d=E_d, E_PC=E_PC,
                                            ht2=h_t2[E_ki]: b.tensor_tensor(
                                    out=y_acc[:, E_d * TC:(E_d + 1) * TC], in0=ht2[:],
                                    in1=E_PC[:, :TC], op=ALU.mult), deps=[E_sc, E_dC])
                            else:
                                pr = E.emit("v", lambda b, E_PC=E_PC, ht2=h_t2[E_ki],
                                            pt2=prod2[E_ki]: b.tensor_tensor(
                                    out=pt2[:], in0=ht2[:], in1=E_PC[:, :TC],
                                    op=ALU.mult),
                                    deps=[E_sc, E_dC, prevs.get(("ac", E_ki))])
                                ac = E.emit("g", lambda b, E_d=E_d, pt2=prod2[E_ki]:
                                            b.tensor_tensor(
                                    out=y_acc[:, E_d * TC:(E_d + 1) * TC],
                                    in0=y_acc[:, E_d * TC:(E_d + 1) * TC], in1=pt2[:],
                                    op=ALU.add), deps=[pr])
                                prevs[("pr", E_ki)] = pr
                            prevs[("ac", E_ki)] = ac
                            ac_by_d[E_d] = ac
                            acc_last = ac
                        col = d * NST + n
                        if tci == first_tci:
                            init = 0.0
                        else:
                            init = carry[:, col:col + 1]
                        scdeps = [sa, bt_, prevs.get(("pr", ki)), prevs.get(("cr", ki))]
                        if p == "f":
                            sc_ = E.emit("v", lambda b, init=init, PA=PA, bt=bt, ht=ht:
                                         b.tensor_tensor_scan(
                                out=ht[:], data0=PA[:, :TC], data1=bt[:], initial=init,
                                op0=ALU.mult, op1=ALU.add), deps=scdeps)
                            cr = E.emit("s", lambda b, col=col, ht=ht: b.copy(
                                carry[:, col:col + 1], ht[:, TC - 1:TC]),
                                deps=[sc_], small=True)
                        else:
                            sc_ = E.emit("v", lambda b, init=init, PA=PA, bt=bt, ht=ht:
                                         b.tensor_tensor_scan(
                                out=ht[:, ::-1], data0=PA[:, :TC][:, ::-1],
                                data1=bt[:, ::-1],
                                initial=init, op0=ALU.mult, op1=ALU.add),
                                deps=scdeps)
                            cr = E.emit("s", lambda b, col=col, ht=ht: b.copy(
                                carry[:, col:col + 1], ht[:, 0:1]),
                                deps=[sc_], small=True)
                        prevs[("sc", ki)] = sc_
                        prevs[("sca", ka)] = sc_
                        prevs[("cr", ki)] = cr
                        pend = (n, d, ki, sc_, PC, dC)
                        kidx += 1
                    prev["bc"][pb] = [E.marker("v"), E.marker("g")]
                if pend is not None:
                    E_n, E_d, E_ki, E_sc, E_PC, E_dC = pend
                    if E_n == 0:
                        ac = E.emit("v", lambda b, E_d=E_d, E_PC=E_PC,
                                    ht2=h_t2[E_ki]: b.tensor_tensor(
                            out=y_acc[:, E_d * TC:(E_d + 1) * TC], in0=ht2[:],
                            in1=E_PC[:, :TC], op=ALU.mult), deps=[E_sc, E_dC])
                    else:
                        pr = E.emit("v", lambda b, E_PC=E_PC, ht2=h_t2[E_ki],
                                    pt2=prod2[E_ki]: b.tensor_tensor(
                            out=pt2[:], in0=ht2[:], in1=E_PC[:, :TC], op=ALU.mult),
                            deps=[E_sc, E_dC, prevs.get(("ac", E_ki))])
                        ac = E.emit("g", lambda b, E_d=E_d, pt2=prod2[E_ki]:
                                    b.tensor_tensor(
                            out=y_acc[:, E_d * TC:(E_d + 1) * TC],
                            in0=y_acc[:, E_d * TC:(E_d + 1) * TC], in1=pt2[:],
                            op=ALU.add), deps=[pr])
                    ac_by_d[E_d] = ac
                    acc_last = ac
                    pend = None
                yp_toks = []
                for d in range(NDC):
                    t1_ = E.emit("v", lambda b, d=d, p=p: b.tensor_scalar(
                        out=prod[:], in0=u_c[:, d * TC:(d + 1) * TC],
                        scalar1=D_sb[p][:, d:d + 1], scalar2=None, op0=ALU.mult),
                        deps=[ac_by_d[d], yp_toks[-1] if yp_toks else
                              [prevs.get(("ac", 0)), prevs.get(("ac", 1))]])
                    t2_ = E.emit("v", lambda b, d=d: b.tensor_tensor(
                        out=prod[:], in0=y_acc[:, d * TC:(d + 1) * TC], in1=prod[:],
                        op=ALU.add), deps=[t1_])
                    t3_ = E.emit("v", lambda b, d=d: b.tensor_tensor(
                        out=yp_bf[:, d * TC:(d + 1) * TC], in0=prod[:],
                        in1=sz_c[:, d * TC:(d + 1) * TC], op=ALU.mult),
                        deps=[t2_, uld_toks[2 * d + 1], prev["x2"]])
                    yp_toks.append(t3_)
                prev["yp"] = yp_toks[-1]
                if debug and yp_h[p] is not None:
                    dtap = None
                    for d in range(NDC):
                        cpy = E.emit("v", lambda b, d=d: b.tensor_copy(
                            prod[:], yp_bf[:, d * TC:(d + 1) * TC]),
                            deps=[prev["yp"], dtap])
                        dtap, _ = E.dma("g", yp_h[p][d * 128:(d + 1) * 128, t0:t0 + TC],
                                        prod[:], deps=[cpy])
                    prev["yp"] = [E.marker("v"), dtap]
                for mq in range(NMC):
                    pb = mq % 2
                    wmm = None
                    for d in range(NDC):
                        wmm = E.emit("t", lambda b, d=d, mq=mq: b.matmul(
                            P[2 + (mq % 2)][:, :TC],
                            woutT[:, d * DM + mq * 128:d * DM + (mq + 1) * 128],
                            yp_bf[:, d * TC:(d + 1) * TC],
                            start=(d == 0), stop=(d == NDC - 1)),
                            deps=[wo_done, yp_toks[d], prev["stg"][pb]], small=True)
                    if p == "f":
                        dx, _ = E.dma("y", xres[pb][:],
                                      xT_h[mq * 128:(mq + 1) * 128, t0:t0 + TC],
                                      deps=[prev["xr"][pb]])
                        sg = E.emit("v", lambda b, pb=pb, mq=mq: b.scalar_tensor_tensor(
                            out=xres[pb][:], in0=xres[pb][:], scalar=2.0,
                            in1=P[2 + pb][:, :TC], op0=ALU.mult, op1=ALU.add),
                            deps=[wmm, dx])
                        dst, _ = E.dma("g", x2pre_h[mq * 128:(mq + 1) * 128, t0:t0 + TC],
                                       xres[pb][:], deps=[sg])
                        prev["stg"][pb] = sg
                        prev["xr"][pb] = dst
                    else:
                        dx, _ = E.dma("y", xres[pb][:],
                                      x2pre_h[mq * 128:(mq + 1) * 128, t0:t0 + TC],
                                      deps=[prev["xr"][pb]])
                        sg = E.emit("v", lambda b, pb=pb, mq=mq: b.tensor_tensor(
                            out=stall[:, mq * TC:(mq + 1) * TC], in0=xres[pb][:],
                            in1=P[2 + pb][:, :TC], op=ALU.add),
                            deps=[wmm, dx, prev["x2"]])
                        prev["stg"][pb] = sg
                        prev["xr"][pb] = sg
                if p == "b":
                    stg_done = E.marker("v")
                    mm_toks2 = []
                    sq_prev2 = None
                    for mq in range(NMC):
                        sq2 = E.emit("s", lambda b, mq=mq: b.activation(
                            out=c_sqbf[:], in_=stall[:, mq * TC:(mq + 1) * TC],
                            func=AF.Square),
                            deps=[stg_done, mm_toks2[-1] if mm_toks2 else None])
                        cp2 = E.emit("v", lambda b, mq=mq: b.tensor_copy(
                            c_xbf[:], stall[:, mq * TC:(mq + 1) * TC]),
                            deps=[stg_done, mm_toks2[-2] if len(mm_toks2) >= 2 else None])
                        mA = E.emit("t", lambda b, mq=mq: b.matmul(
                            P[4][0:1, :TC], onesb[:], c_xbf[:],
                            start=(mq == 0), stop=(mq == NMC - 1)),
                            deps=[cp2, prev["x2"]], small=True)
                        mB = E.emit("t", lambda b, mq=mq: b.matmul(
                            P[5][0:1, :TC], onesb[:], c_sqbf[:],
                            start=(mq == 0), stop=(mq == NMC - 1)),
                            deps=[sq2, prev["x2"]], small=True)
                        mm_toks2 += [mA, mB]
                    tm = E.emit("v", lambda b: b.tensor_scalar(
                        out=c_stm[:], in0=P[4][0:1, :TC], scalar1=1.0 / DM,
                        scalar2=None, op0=ALU.mult), deps=mm_toks2, small=True)
                    te = E.emit("v", lambda b: b.tensor_scalar(
                        out=c_ste[:], in0=P[5][0:1, :TC], scalar1=1.0 / DM,
                        scalar2=None, op0=ALU.mult), deps=mm_toks2, small=True)
                    t2m = E.emit("v", lambda b: b.tensor_tensor(
                        out=c_stt[:], in0=c_stm[:], in1=c_stm[:], op=ALU.mult),
                        deps=[tm], small=True)
                    tv = E.emit("v", lambda b: b.tensor_tensor(
                        out=c_ste[:], in0=c_ste[:], in1=c_stt[:], op=ALU.subtract),
                        deps=[te, t2m], small=True)
                    tve = E.emit("v", lambda b: b.tensor_scalar(
                        out=c_ste[:], in0=c_ste[:], scalar1=float(EPS), scalar2=None,
                        op0=ALU.add), deps=[tv], small=True)
                    tsd = E.emit("s", lambda b: b.activation(
                        out=c_stt[:], in_=c_ste[:], func=AF.Sqrt), deps=[tve], small=True)
                    trs = E.emit("v", lambda b: b.reciprocal(c_str[:], c_stt[:]),
                                 deps=[tsd], small=True)
                    bm = E.emit("t", lambda b: b.matmul(
                        P[4][:, :TC], ones_rt[:], c_stm[:], start=True, stop=True),
                        deps=[tm, t2m, mm_toks2, prev["bc"][0]], small=True)
                    br = E.emit("t", lambda b: b.matmul(
                        P[5][:, :TC], ones_rt[:], c_str[:], start=True, stop=True),
                        deps=[trs, prev["bc"][1]], small=True)
                    x2_toks = []
                    for mq in range(NMC):
                        n1 = E.emit("v", lambda b, mq=mq: b.tensor_tensor(
                            out=x2_c[:, mq * TC:(mq + 1) * TC],
                            in0=stall[:, mq * TC:(mq + 1) * TC], in1=P[4][:, :TC],
                            op=ALU.subtract), deps=[bm])
                        n2 = E.emit("v", lambda b, mq=mq: b.tensor_tensor(
                            out=x2_c[:, mq * TC:(mq + 1) * TC],
                            in0=x2_c[:, mq * TC:(mq + 1) * TC], in1=P[5][:, :TC],
                            op=ALU.mult), deps=[br, n1])
                        n3 = E.emit("v", lambda b, mq=mq: b.tensor_copy(
                            x2bf_c[:, mq * TC:(mq + 1) * TC],
                            x2_c[:, mq * TC:(mq + 1) * TC]), deps=[n2])
                        E.dma("g", x2_h[mq * 128:(mq + 1) * 128, t0:t0 + TC],
                              x2_c[:, mq * TC:(mq + 1) * TC], deps=[n2])
                        E.dma("g", x2bf_h[mq * 128:(mq + 1) * 128, t0:t0 + TC],
                              x2bf_c[:, mq * TC:(mq + 1) * TC], deps=[n3])
                        x2_toks.append(n3)
                    rt_last = prev["rt"]
                    for tb2 in range(TC // 128):
                        tbl = (t0 + tb2 * 128) // 128
                        lg = None
                        for mq in range(NMC):
                            lg = E.emit("t", lambda b, mq=mq, tb2=tb2: b.matmul(
                                P[6][:, :NE],
                                x2_c[:, mq * TC + tb2 * 128:mq * TC + tb2 * 128 + 128],
                                rw_sb[:, mq * NE:(mq + 1) * NE],
                                start=(mq == 0), stop=(mq == NMC - 1)),
                                deps=[x2_toks[mq], d_rw, rt_last, prev["bc"][0]],
                                small=True)
                        g1 = E.emit("v", lambda b: b.tensor_tensor(
                            out=gt_ps[:], in0=P[6][:, :NE], in1=rb_bc[:], op=ALU.add),
                            deps=[lg, d_rb], small=True)
                        g2 = E.emit("s", lambda b: b.activation(
                            out=gt_ps[:], in_=gt_ps[:], func=AF.Exp), deps=[g1], small=True)
                        g3 = E.emit("v", lambda b: b.reduce_sum(
                            sum_ps[:], gt_ps[:], axis=AX.X), deps=[g2], small=True)
                        g4 = E.emit("v", lambda b: b.reciprocal(m1_ps[:], sum_ps[:]),
                                    deps=[g3], small=True)
                        g5 = E.emit("v", lambda b, tbl=tbl: b.tensor_scalar(
                            out=gates_sb[:, tbl * NE:(tbl + 1) * NE], in0=gt_ps[:],
                            scalar1=m1_ps[:], scalar2=None, op0=ALU.mult),
                            deps=[g4], small=True)
                        m1t = E.emit("v", lambda b, tbl=tbl: b.reduce_max(
                            m1_ps[:], gates_sb[:, tbl * NE:(tbl + 1) * NE], axis=AX.X),
                            deps=[g5], small=True)
                        e1t = E.emit("v", lambda b, tbl=tbl: b.tensor_scalar(
                            out=eq_ps[:], in0=gates_sb[:, tbl * NE:(tbl + 1) * NE],
                            scalar1=m1_ps[:], scalar2=None, op0=ALU.is_ge),
                            deps=[m1t], small=True)
                        e2t = E.emit("v", lambda b, tbl=tbl: b.scalar_tensor_tensor(
                            out=eq_ps[:], in0=eq_ps[:], scalar=-1e30,
                            in1=gates_sb[:, tbl * NE:(tbl + 1) * NE],
                            op0=ALU.mult, op1=ALU.add), deps=[e1t], small=True)
                        m2t = E.emit("v", lambda b: b.reduce_max(
                            m2_ps[:], eq_ps[:], axis=AX.X), deps=[e2t], small=True)
                        msk = E.emit("v", lambda b, tbl=tbl: b.tensor_scalar(
                            out=mask_sb[:, tbl * NE:(tbl + 1) * NE],
                            in0=gates_sb[:, tbl * NE:(tbl + 1) * NE],
                            scalar1=m2_ps[:], scalar2=None, op0=ALU.is_ge),
                            deps=[m2t], small=True)
                        rt_last = msk
                    prev["rt"] = rt_last
                    prev["x2"] = [E.marker("v"), E.marker("s")] + E.dmarker("st")

        pool_c.close()
        pool_p2.close()
        E.barrier()
        if stop_after in (None, "moe"):

            MTC = 512
            NMTC = T // MTC
            pool_r = ExitStack()
            runb = sb_in(pool_r, "runb", [1, NE])
            base_r = sb_in(pool_r, "base_r", [1, NE])
            cnt_r = sb_in(pool_r, "cnt_r", [8, NE])
            pm_sb = sb_in(pool_r, "pm_sb", [8, 1])
            le_r = sb_in(pool_r, "le_r", [128, NE])
            pos_sb = sb_in(pool_r, "pos_sb", [128, NE])
            val_r = sb_in(pool_r, "val_r", [128, NE])
            wT_sb = sb_in(pool_r, "wT_sb", [8, T], BF16)
            sel8_sb = sb_in(pool_r, "sel8_sb", [8, 8 * 128], BF16)

            d_pm, _ = E.dma("y", pm_sb[:], I["prefmask"])
            d_sel8, _ = E.dma("y", sel8_sb[:], sel8)
            ones32c = sb_in(pool_r, "ones32c", [128, 1])
            oz = E.emit("v", lambda b: b.memset(ones32c[:], 1.0), small=True)
            ctok = None
            for tb in range(NTB):
                ctok = E.emit("t", lambda b, tb=tb: b.matmul(
                    P[3][0:1, :NE], ones32c[:], mask_sb[:, tb * NE:(tb + 1) * NE],
                    start=(tb == 0), stop=(tb == NTB - 1)), deps=[oz, ctok], small=True)
            cnt_ev = E.emit("v", lambda b: b.tensor_copy(runb[:], P[3][0:1, :NE]),
                            deps=[ctok], small=True)
            d_ci, _ = E.dma("g", cc_in, runb[:], deps=[cnt_ev])
            if fake_cc:
                cc, _ = E.dma("g", cc_out.squeeze(0).rearrange("(c e) -> c e", e=NE)[0:1, :],
                              cc_in, deps=[d_ci])
                z8 = E.emit("v", lambda b: b.memset(cnt_r[:], 0.0), deps=[], small=True)
                d_co, _ = E.dma("g", cnt_r[0:1, :],
                                cc_out.squeeze(0).rearrange("(c e) -> c e", e=NE)[0:1, :],
                                deps=[cc, z8])
            else:
                cc = E.emit("g", lambda b: b.collective_compute(
                    "AllGather", ALU.bypass, replica_groups=[[0, 1, 2, 3, 4, 5, 6, 7]],
                    ins=[cc_in], outs=[cc_out]), deps=[d_ci], small=True)
                d_co, _ = E.dma("g", cnt_r[:],
                                cc_out.squeeze(0).rearrange("(c e) -> c e", e=NE),
                                deps=[cc])
            bmm = E.emit("t", lambda b: b.matmul(
                P[1][0:1, :NE], pm_sb[:], cnt_r[:], start=True, stop=True),
                deps=[d_co, d_pm, cnt_ev], small=True)
            bev = E.emit("v", lambda b: b.tensor_copy(base_r[:], P[1][0:1, :NE]),
                         deps=[bmm], small=True)
            rtok = E.emit("v", lambda b: b.tensor_copy(runb[:], base_r[:]),
                          deps=[bev], small=True)
            wtok = None
            c2tok = bev
            for tb in range(NTB):
                m1_ = E.emit("t", lambda b, tb=tb: b.matmul(
                    P[1][:, :NE], trit[:], mask_sb[:, tb * NE:(tb + 1) * NE],
                    start=True, stop=False), deps=[d_tri, rtok, wtok], small=True)
                m2_ = E.emit("t", lambda b: b.matmul(
                    P[1][:, :NE], ones_rt[:], runb[:], start=False, stop=True),
                    deps=[m1_], small=True)
                cpp = E.emit("v", lambda b: b.tensor_copy(pos_sb[:], P[1][:, :NE]),
                             deps=[m2_], small=True)
                c2tok = E.emit("t", lambda b, tb=tb: b.matmul(
                    P[4][0:1, :NE], ones32c[:], mask_sb[:, tb * NE:(tb + 1) * NE],
                    start=True, stop=True), deps=[oz, c2tok, cpp], small=True)
                rtok = E.emit("v", lambda b: b.tensor_tensor(
                    out=runb[:], in0=runb[:], in1=P[4][0:1, :NE], op=ALU.add),
                    deps=[c2tok], small=True)
                l1_ = E.emit("v", lambda b: b.tensor_scalar(
                    out=le_r[:], in0=pos_sb[:], scalar1=float(CAP) + 0.5, scalar2=None,
                    op0=ALU.is_le), deps=[cpp], small=True)
                v1_ = E.emit("v", lambda b, tb=tb: b.tensor_tensor(
                    out=val_r[:], in0=mask_sb[:, tb * NE:(tb + 1) * NE], in1=le_r[:],
                    op=ALU.mult), deps=[l1_], small=True)
                w1_ = E.emit("v", lambda b, tb=tb: b.tensor_tensor(
                    out=w_sb[:, tb * NE:(tb + 1) * NE], in0=val_r[:],
                    in1=gates_sb[:, tb * NE:(tb + 1) * NE], op=ALU.mult),
                    deps=[v1_], small=True)
                wt_ = E.emit("t", lambda b, tb=tb: b.matmul(
                    P[2][0:NE, :128], w_sb[:, tb * NE:(tb + 1) * NE], identt[:],
                    start=True, stop=True), deps=[w1_, wtok], small=True)
                wtok = E.emit("v", lambda b, tb=tb: b.tensor_copy(
                    wT_sb[:, tb * 128:(tb + 1) * 128], P[2][0:NE, :128]),
                    deps=[wt_], small=True)
            w_done = wtok
            if debug and rt_h is not None:
                E.dma("g", rt_h[:, 0:NTB * NE], gates_sb[:], deps=[w_done])
                E.dma("g", rt_h[:, NTB * NE:2 * NTB * NE], mask_sb[:], deps=[w_done])
                E.dma("g", rt_h[:, 2 * NTB * NE:3 * NTB * NE], w_sb[:], deps=[w_done])

            wld = [sb_in(pool_r, f"wld{i}", [128, DH // 2]) for i in range(2)]
            ym_acc = [sb_in(pool_r, f"ym_acc{i}", [128, MTC]) for i in range(2)]
            pool_m = ExitStack()
            w1bf = sb_in(pool_m, "w1bf", [128, NMC * DH], BF16)
            w2bf = sb_in(pool_m, "w2bf", [128, 32 * DM], BF16)
            x2bf_s = sb_in(pool_m, "x2bf_s", [128, NMC * MTC], BF16)
            h_bf = sb_in(pool_m, "h_bf", [128, 32 * MTC], BF16)
            b1T = sb_in(pool_m, "b1T", [128, 32 * NE])
            b2_s = sb_in(pool_m, "b2_s", [8, DM], BF16)

            for _e in range(NE):
                d_b1, _ = E.dma("y",
                                b1T[:].rearrange("p (h e) -> p h e", h=32)[:, :, _e],
                                I["b1"][_e].rearrange("(h p) -> p h", p=128),
                                slow=True)
            d_b2, _ = E.dma("g", b2_s[:], I["b2"])
            ib_toks = {}
            prev_ib_ev = None
            for tci in range(NMTC):
                t0 = tci * MTC
                for mq in range(NMC):
                    ib = E.emit("t", lambda b, mq=mq, t0=t0: b.matmul(
                        P[3][:, :MTC], b2_s[:, mq * 128:(mq + 1) * 128], wT_sb[:, t0:t0 + MTC],
                        start=True, stop=True), deps=[d_b2, w_done, prev_ib_ev],
                        small=True)
                    ya = ym_acc[(tci * NMC + mq) % 2]
                    ev_ = E.emit("v", lambda b, ya=ya: b.tensor_copy(
                        ya[:], P[3][:, :MTC]),
                        deps=[ib, ib_toks.get((tci * NMC + mq - 2, "roll"))])
                    dst, _ = E.dma("g", ymoe_h[mq * 128:(mq + 1) * 128, t0:t0 + MTC],
                                   ya[:], deps=[ev_])
                    ib_toks[(tci, mq)] = dst
                    ib_toks[(tci * NMC + mq, "roll")] = dst
                    prev_ib_ev = ev_
            prev_ymm = [E.marker("v"), E.marker("v")]
            mm_h = None
            prev_wb = None
            for e in range(NE):
                lasth = mm_h if e else None
                w1toks = []
                for mq in range(NMC):
                    dw, _ = E.dma("g", w1bf[:, mq * DH:(mq + 1) * DH],
                                  I["w1"][e, mq * 128:(mq + 1) * 128, :],
                                  deps=[lasth])
                    w1toks.append(dw)
                w2toks = []
                for hq in range(32):
                    dw, _ = E.dma("g", w2bf[:, hq * DM:(hq + 1) * DM],
                                  I["w2"][e, hq * 128:(hq + 1) * 128, :],
                                  deps=[prev_ymm[0], prev_ymm[1]])
                    w2toks.append(dw)
                wcast = [w1toks, w2toks]
                for tci in range(NMTC):
                    t0 = tci * MTC
                    x2l = []
                    for mq in range(NMC):
                        dx2, _ = E.dma("y", x2bf_s[:, mq * MTC:(mq + 1) * MTC],
                                       x2bf_h[mq * 128:(mq + 1) * 128, t0:t0 + MTC],
                                       deps=[mm_h])
                        x2l.append(dx2)
                    wb = E.emit("t", lambda b, e=e, t0=t0: b.matmul(
                        P[4][:, :MTC], sel8_sb[:, e * 128:(e + 1) * 128], wT_sb[:, t0:t0 + MTC],
                        start=True, stop=True), deps=[w_done, d_sel8, prev_wb], small=True)
                    wbe = wb
                    gel_toks = []
                    for hq in range(32):
                        pb2 = hq % 2
                        mm_h = None
                        for mq in range(NMC):
                            mm_h = E.emit("t", lambda b, hq=hq, mq=mq: b.matmul(
                                P[0 + (hq % 2)][:, :MTC],
                                w1bf[:, mq * DH + hq * 128:mq * DH + (hq + 1) * 128],
                                x2bf_s[:, mq * MTC:(mq + 1) * MTC],
                                start=(mq == 0), stop=(mq == NMC - 1)),
                                deps=[wcast, x2l[mq],
                                      gel_toks[-2] if len(gel_toks) >= 2 else None],
                                small=True)
                        ge = E.emit("s", lambda b, hq=hq, e=e: b.activation(
                            out=h_bf[:, hq * MTC:(hq + 1) * MTC], in_=P[0 + (hq % 2)][:, :MTC],
                            func=AF.Gelu, bias=b1T[:, hq * NE + e:hq * NE + e + 1]),
                            deps=[mm_h], small=True)
                        gel_toks.append(ge)
                    for mq in range(NMC):
                        mm_y = None
                        for hq in range(32):
                            mm_y = E.emit("t", lambda b, hq=hq, mq=mq: b.matmul(
                                P[2 + (mq % 2)][:, :MTC],
                                w2bf[:, hq * DM + mq * 128:hq * DM + (mq + 1) * 128],
                                h_bf[:, hq * MTC:(hq + 1) * MTC],
                                start=(hq == 0), stop=(hq == 31)),
                                deps=[gel_toks[hq], prev_ymm[mq % 2]], small=True)
                        ya = ym_acc[(tci * NMC + mq) % 2]
                        cy = E.emit("s", lambda b, ya=ya, mq=mq: b.copy(
                            ya[:], P[2 + (mq % 2)][:, :MTC]),
                            deps=[mm_y, ib_toks.get(((tci * NMC + mq - 2) % (NMTC * NMC), "roll"))])
                        ct = E.emit("v", lambda b, ya=ya: b.tensor_tensor(
                            out=ya[:], in0=ya[:], in1=P[4][:, :MTC], op=ALU.mult),
                            deps=[cy, wbe])
                        prev_ymm[mq % 2] = ct
                        prev_wb = ct
                        dacc, _ = E.dma("g", ymoe_h[mq * 128:(mq + 1) * 128, t0:t0 + MTC],
                                        ya[:], deps=[ct, ib_toks[(tci, mq)]], accum=True)
                        ib_toks[(tci, mq)] = dacc
                        ib_toks[(tci * NMC + mq, "roll")] = dacc

            pool_m.close()
            E.barrier()
            f_x2 = sb_in(pool_r, "f_x2", [128, NMC * MTC])
            f_stg = sb_in(pool_r, "f_stg", [128, DM])
            f_sq = sb_in(pool_r, "f_sq", [128, MTC], BF16)
            f_xb = sb_in(pool_r, "f_xb", [128, MTC], BF16)
            f_m = sb_in(pool_r, "f_m", [1, MTC]); f_e2 = sb_in(pool_r, "f_e2", [1, MTC])
            f_t = sb_in(pool_r, "f_t", [1, MTC]); f_r = sb_in(pool_r, "f_r", [1, MTC])
            prev_f = {"ld": None, "mm": None, "st": None, "tp": None}
            for tci in range(NMTC):
                t0 = tci * MTC
                ld_toks = []
                for mq in range(NMC):
                    wl = wld[mq % 2]
                    d1, _ = E.dma("y", f_x2[:, mq * MTC:(mq + 1) * MTC],
                                  x2_h[mq * 128:(mq + 1) * 128, t0:t0 + MTC],
                                  deps=[prev_f["mm"]])
                    d2, _ = E.dma("y", wl[:, :MTC],
                                  ymoe_h[mq * 128:(mq + 1) * 128, t0:t0 + MTC],
                                  deps=[prev_f["mm"],
                                        ld_toks[mq - 2] if mq >= 2 else None])
                    ad = E.emit("v", lambda b, mq=mq, wl=wl: b.tensor_tensor(
                        out=f_x2[:, mq * MTC:(mq + 1) * MTC],
                        in0=f_x2[:, mq * MTC:(mq + 1) * MTC],
                        in1=wl[:, :MTC], op=ALU.add),
                        deps=[d1, d2])
                    ld_toks.append(ad)
                mmt = []
                for mq in range(NMC):
                    sq2 = E.emit("s", lambda b, mq=mq: b.activation(
                        out=f_sq[:], in_=f_x2[:, mq * MTC:(mq + 1) * MTC], func=AF.Square),
                        deps=[ld_toks[mq], mmt[-1] if mmt else None])
                    cp2 = E.emit("v", lambda b, mq=mq: b.tensor_copy(
                        f_xb[:], f_x2[:, mq * MTC:(mq + 1) * MTC]),
                        deps=[ld_toks[mq], mmt[-2] if len(mmt) >= 2 else None])
                    mA = E.emit("t", lambda b, mq=mq: b.matmul(
                        P[5][0:1, :MTC], onesb[:], f_xb[:],
                        start=(mq == 0), stop=(mq == NMC - 1)),
                        deps=[cp2, prev_f["mm"]], small=True)
                    mB = E.emit("t", lambda b, mq=mq: b.matmul(
                        P[6][0:1, :MTC], onesb[:], f_sq[:],
                        start=(mq == 0), stop=(mq == NMC - 1)),
                        deps=[sq2, prev_f["mm"]], small=True)
                    mmt += [mA, mB]
                tm = E.emit("v", lambda b: b.tensor_scalar(
                    out=f_m[:], in0=P[5][0:1, :MTC], scalar1=1.0 / DM, scalar2=None,
                    op0=ALU.mult), deps=mmt, small=True)
                te = E.emit("v", lambda b: b.tensor_scalar(
                    out=f_e2[:], in0=P[6][0:1, :MTC], scalar1=1.0 / DM, scalar2=None,
                    op0=ALU.mult), deps=mmt, small=True)
                t2m = E.emit("v", lambda b: b.tensor_tensor(
                    out=f_t[:], in0=f_m[:], in1=f_m[:], op=ALU.mult), deps=[tm], small=True)
                tv = E.emit("v", lambda b: b.tensor_tensor(
                    out=f_e2[:], in0=f_e2[:], in1=f_t[:], op=ALU.subtract),
                    deps=[te, t2m], small=True)
                tve = E.emit("v", lambda b: b.tensor_scalar(
                    out=f_e2[:], in0=f_e2[:], scalar1=float(EPS), scalar2=None, op0=ALU.add),
                    deps=[tv], small=True)
                tsd = E.emit("s", lambda b: b.activation(
                    out=f_t[:], in_=f_e2[:], func=AF.Sqrt), deps=[tve], small=True)
                trs = E.emit("v", lambda b: b.reciprocal(f_r[:], f_t[:]), deps=[tsd], small=True)
                bm = E.emit("t", lambda b: b.matmul(
                    P[5][:, :MTC], ones_rt[:], f_m[:], start=True, stop=True),
                    deps=[tm, t2m, mmt, prev_f["tp"]], small=True)
                br = E.emit("t", lambda b: b.matmul(
                    P[6][:, :MTC], ones_rt[:], f_r[:], start=True, stop=True),
                    deps=[trs, prev_f["tp"]], small=True)
                x3toks = []
                for mq in range(NMC):
                    n1 = E.emit("v", lambda b, mq=mq: b.tensor_tensor(
                        out=f_x2[:, mq * MTC:(mq + 1) * MTC],
                        in0=f_x2[:, mq * MTC:(mq + 1) * MTC], in1=P[5][:, :MTC],
                        op=ALU.subtract), deps=[bm])
                    n2 = E.emit("v", lambda b, mq=mq: b.tensor_tensor(
                        out=f_x2[:, mq * MTC:(mq + 1) * MTC],
                        in0=f_x2[:, mq * MTC:(mq + 1) * MTC], in1=P[6][:, :MTC],
                        op=ALU.mult), deps=[br, n1])
                    x3toks.append(n2)
                for tb2 in range(MTC // 128):
                    tp_toks = []
                    for mq in range(NMC):
                        tp = E.emit("t", lambda b, mq=mq, tb2=tb2: b.matmul(
                            P[7][:, :128],
                            f_x2[:, mq * MTC + tb2 * 128:mq * MTC + tb2 * 128 + 128],
                            identt[:], start=True, stop=True),
                            deps=[x3toks[mq], prev_f["st"], tp_toks[-1] if tp_toks else None],
                            small=True)
                        ev2 = E.emit("v", lambda b, mq=mq: b.tensor_copy(
                            f_stg[:, mq * 128:(mq + 1) * 128], P[7][:, :128]),
                            deps=[tp], small=True)
                        tp_toks.append(ev2)
                    dout, _ = E.dma("g", out_h[t0 + tb2 * 128:t0 + tb2 * 128 + 128, :],
                                    f_stg[:], deps=[tp_toks[-1]])
                    prev_f["st"] = dout
                prev_f["tp"] = E.marker("t")
                prev_f["mm"] = [E.marker("v"), E.marker("s"), prev_f["st"]]
            pool_r.close()


        E.barrier()
        pool_tail = ExitStack()
        if stop_after is not None:
            ztile = sb_in(pool_tail, "ztile", [128, 256])
            z = E.emit("v", lambda b: b.memset(ztile[:, :], 0.0),
                       deps=[E.dmarker("all")])
            for tb in range(NTB):
                for j in range(4):
                    E.dma("g", out_h[tb * 128:(tb + 1) * 128, j * 256:(j + 1) * 256],
                          ztile[:, :], deps=[z], queue="st")
        fin = E.dmarker("all")
        E.emit("g", lambda b: b.engine_nop(), deps=[fin], small=True)
        pool_tail.close()

        E.finalize(block, sems, dsems)

    return nc


def make_in_maps(inputs):
    ren = {}
    for k, v in inputs.items():
        if k == "L0_attn_x":
            ren["x_full"] = np.ascontiguousarray(np.asarray(v))
        elif k.startswith("L0_attn_"):
            ren[k[len("L0_attn_"):]] = np.ascontiguousarray(np.asarray(v))
        elif k.startswith("L0_ffn_"):
            ren[k[len("L0_ffn_"):]] = np.ascontiguousarray(np.asarray(v))
    in_maps = []
    for c in range(8):
        m = {"x": ren["x_full"][c]}
        for p in ("f", "b"):
            for nm in ("Win", "convb", "Wx", "Wdt", "bdt", "Alog", "D", "Wout"):
                m[f"{p}_{nm}"] = ren[f"{p}_{nm}"]
            m[f"{p}_convw"] = np.ascontiguousarray(ren[f"{p}_convw"].reshape(DI, 4))
        for nm in ("router_w", "router_b", "w1", "b1", "w2", "b2"):
            m[nm] = ren[nm]
        m["norm_g"] = ren["norm_g"]
        m["norm_b"] = ren["norm_b"]
        m["prefmask"] = (np.arange(NE) < c).astype(np.float32).reshape(NE, 1)
        in_maps.append(m)
    return in_maps


def kernel(**inputs):
    nc = build_nc(debug=False)
    in_maps = make_in_maps(inputs)
    res = run_bass_kernel_spmd(nc, in_maps, core_ids=list(range(8)))
    return np.stack([r["out"] for r in res.results])


# version 15 (final)
# speedup vs baseline: 1.1327x; 1.1327x over previous
import numpy as np
import bass_rust
import concourse.bass as bass
import concourse.mybir as mybir
from concourse.bass_utils import run_bass_kernel_spmd
from contextlib import ExitStack

F32 = mybir.dt.float32
BF16 = mybir.dt.bfloat16
AF = mybir.ActivationFunctionType
ALU = mybir.AluOpType
AX = mybir.AxisListType

T = 4096
DM = 1024
DI = 2048
NST = 16
RK = 64
NE = 8
DH = 4096
NMC = DM // 128
NDC = DI // 128
EPS = 1e-5
CAP = 5120
TC = 256
NTC = T // TC
NTB = T // 128

ENGS = ("v", "s", "t", "g", "y")


DMA_SLOTS = {"y": 10, "g": 8, "s": 2, "v": 2}


class Emitter:

    def __init__(self):
        self.ops = {e: [] for e in ENGS}
        self.cnt = {e: 0 for e in ENGS}
        self.slotcnt = {(e, i): 0 for e, n in DMA_SLOTS.items() for i in range(n)}
        self.slot_rr = {e: 0 for e in DMA_SLOTS}
        self.pending = {e: {} for e in ENGS}
        self.waited = {e: {} for e in ENGS}

    def _prune(self, eng, waits):
        w = self.waited[eng]
        out = {}
        for k, v in waits.items():
            if k == ("e", eng):
                pass
            if v > w.get(k, 0):
                out[k] = v
                w[k] = v
        return out

    def barrier(self):
        toks = [self.marker(e) for e in ENGS if self.cnt[e] > 0] + self.dmarker("all")
        w = self._waits(toks)
        for e in ENGS:
            for k, v in w.items():
                self.pending[e][k] = max(self.pending[e].get(k, 0), v)

    def _take_pending(self, eng, waits):
        for k, v in self.pending[eng].items():
            waits[k] = max(waits.get(k, 0), v)
        self.pending[eng] = {}
        return waits

    @classmethod
    def _flat(cls, deps):
        for tok in deps:
            if tok is None:
                continue
            if isinstance(tok, list):
                yield from cls._flat(tok)
            else:
                yield tok

    def _waits(self, deps, eng=None):
        waits = {}
        for tok in self._flat(deps):
            if tok[0] == "e":
                _, de, dc, dsmall = tok
                key = ("e", de)
            else:
                _, slot, dc = tok
                key = ("d", slot)
            waits[key] = max(waits.get(key, 0), dc)
        return waits

    def emit(self, eng, fn, deps=(), small=False):
        waits = self._prune(eng, self._take_pending(eng, self._waits(deps, eng)))
        self.cnt[eng] += 1

        def closure(b, waits=waits, fn=fn, eng=eng):
            items = [(self.sems[k[1]] if k[0] == "e" else self.dsems[k[1]], v)
                     for k, v in waits.items()]
            for sem, val in items[:-1]:
                b.wait_ge(sem, val)
            inst = fn(b)
            inst.then_inc(self.sems[eng], 1)
            if items:
                inst.wait_op(items[-1][0], items[-1][1], "sem-ge")
        self.ops[eng].append(closure)
        return ("e", eng, self.cnt[eng], small)

    def dma(self, eng, out, in_, deps=(), queue=None, slow=False, accum=False):
        si = self.slot_rr[eng]
        self.slot_rr[eng] = (si + 1) % DMA_SLOTS[eng]
        slot = (eng, si)
        prior = self.slotcnt[slot]
        self.slotcnt[slot] = prior + 16
        waits = self._prune(eng, self._take_pending(eng, self._waits(deps)))
        if prior and prior > self.waited[eng].get(("d", slot), 0):
            self.waited[eng][("d", slot)] = prior

        def closure(b, waits=waits, out=out, in_=in_, slow=slow, slot=slot,
                    prior=prior, accum=accum):
            for (kind, name), val in waits.items():
                b.wait_ge(self.sems[name] if kind == "e" else self.dsems[name], val)
            if prior:
                b.wait_ge(self.dsems[slot], prior)
            kw = dict(allow_slow_non_contiguous=slow)
            if accum:
                kw["accum_op"] = mybir.AluOpType.add
            b.dma_start(out=out, in_=in_, **kw).then_inc(self.dsems[slot], 16)
        self.ops[eng].append(closure)
        return ("d", slot, prior + 16), None

    def marker(self, eng):
        return ("e", eng, self.cnt[eng], True)

    def dmarker(self, queue):
        engs = {"ld": ("y", "s", "v"), "st": ("g",), "all": tuple(DMA_SLOTS)}[queue]
        return [("d", (e, i), self.slotcnt[(e, i)])
                for e in engs for i in range(DMA_SLOTS[e])
                if self.slotcnt[(e, i)] > 0]

    def finalize(self, block, sems, dsems):
        self.sems = sems
        self.dsems = dsems
        for eng, dec in (("v", block.vector), ("s", block.scalar),
                         ("t", block.tensor), ("g", block.gpsimd),
                         ("y", block.sync)):
            if not self.ops[eng]:
                continue

            def make(lst):
                def _(b):
                    for f in lst:
                        f(b)
                return _
            dec(make(self.ops[eng]))


def build_nc(debug=False, stop_after=None, fake_cc=False):
    nc = bass.Bass("TRN2", target_bir_lowering=False)

    def din(name, shape):
        return nc.dram_tensor(name, list(shape), F32, kind="ExternalInput").ap()

    I = {}
    I["x"] = din("x", (T, DM))
    I["norm_g"] = din("norm_g", (DM,))
    I["norm_b"] = din("norm_b", (DM,))
    for p in ("f", "b"):
        for nm, sh in (("Win", (2 * DI, DM)), ("convw", (DI, 4)), ("convb", (DI,)),
                       ("Wx", (RK + 2 * NST, DI)), ("Wdt", (DI, RK)), ("bdt", (DI,)),
                       ("Alog", (DI, NST)), ("D", (DI,)), ("Wout", (DM, DI))):
            I[f"{p}_{nm}"] = din(f"{p}_{nm}", sh)
    I["router_w"] = din("router_w", (DM, NE))
    I["router_b"] = din("router_b", (NE,))
    I["w1"] = din("w1", (NE, DM, DH))
    I["b1"] = din("b1", (NE, DH))
    I["w2"] = din("w2", (NE, DH, DM))
    I["b2"] = din("b2", (NE, DM))
    I["prefmask"] = din("prefmask", (NE, 1))

    out_h = nc.dram_tensor("out", [T, DM], F32, kind="ExternalOutput").ap()

    scratch_kind = "ExternalOutput" if debug else "Internal"

    def scr(name, shape, dt=F32):
        return nc.dram_tensor(name, list(shape), dt, kind=scratch_kind).ap()

    xT_h = scr("xT_h", (DM, T))
    xn_h = scr("xn_h", (DM, T)) if debug else None
    u_h = {p: scr(f"u_{p}_h", (DI, T)) for p in ("f", "b")}
    sz_h = {p: scr(f"sz_{p}_h", (DI, T)) for p in ("f", "b")}
    xdbl_h = {p: scr(f"xdbl_{p}_h", (96, T)) for p in ("f", "b")} if debug else None
    yp_h = {p: (scr(f"yp_{p}_h", (DI, T)) if debug else None) for p in ("f", "b")}
    x2pre_h = scr("x2pre_h", (DM, T))
    x2_h = scr("x2_h", (DM, T))
    x2bf_h = nc.dram_tensor("x2bf_h", [DM, T], BF16, kind="Internal").ap()
    ymoe_h = scr("ymoe_h", (DM, T))
    rt_h = scr("rt_h", (128, NTB * NE * 3)) if debug else None
    cc_in = nc.dram_tensor("cc_in", [1, NE], F32, kind="Internal").ap()
    cc_out = nc.dram_tensor("cc_out", [1, NE * 8], F32, kind="Internal",
                            addr_space=("Local" if fake_cc else "Shared")).ap()

    ident = nc.inline_tensor(np.eye(128, dtype=np.float32), "ident").ap()
    ones_r = nc.inline_tensor(np.ones((1, 128), dtype=np.float32), "ones_r").ap()
    tri = nc.inline_tensor(
        (np.arange(128)[:, None] <= np.arange(128)[None, :]).astype(np.float32), "tri").ap()
    import ml_dtypes
    _sel = np.zeros((96, 32 * 128), dtype=ml_dtypes.bfloat16)
    for _n in range(NST):
        _sel[64 + _n, (2 * _n) * 128:(2 * _n + 1) * 128] = 1.0
        _sel[80 + _n, (2 * _n + 1) * 128:(2 * _n + 2) * 128] = 1.0
    selbc = nc.inline_tensor(_sel, "selbc").ap()
    _sel8 = np.zeros((8, 8 * 128), dtype=ml_dtypes.bfloat16)
    for _e in range(8):
        _sel8[_e, _e * 128:(_e + 1) * 128] = 1.0
    sel8 = nc.inline_tensor(_sel8, "sel8").ap()

    E = Emitter()

    with ExitStack() as ctx:
        def sb_in(pool, name, shape, dt=F32):
            return pool.enter_context(nc.sbuf_tensor(name, shape, dt))

        def sb(name, shape, dt=F32):
            return ctx.enter_context(nc.sbuf_tensor(name, shape, dt))

        def psum(name, shape, dt=F32):
            return ctx.enter_context(nc.psum_tensor(name, shape, dt))

        block = ctx.enter_context(nc.Block())
        sems = {e: ctx.enter_context(nc.semaphore(f"ch_{e}")) for e in ENGS}
        dsems = {(e, i): ctx.enter_context(nc.semaphore(f"dma_{e}{i}"))
                 for e, n in DMA_SLOTS.items() for i in range(n)}

        identt = sb("identt", [128, 128])
        ones_rt = sb("ones_rt", [1, 128])
        trit = sb("trit", [128, 128])
        onesb = sb("onesb", [128, 1], BF16)
        ones_rbf = sb("ones_rbf", [1, 128], BF16)
        g_pp = sb("g_pp", [128, NMC])
        b_pp = sb("b_pp", [128, NMC])
        gates_sb = sb("gates_sb", [128, NTB * NE])
        mask_sb = sb("mask_sb", [128, NTB * NE])
        w_sb = sb("w_sb", [128, NTB * NE])
        pool_p2 = ExitStack()
        wdtT = {p: sb_in(pool_p2, f"wdtT{p}", [64, NDC * 128], BF16) for p in ("f", "b")}
        A_sb = {p: sb_in(pool_p2, f"A_sb{p}", [128, NDC * NST]) for p in ("f", "b")}
        D_sb = {p: sb_in(pool_p2, f"D_sb{p}", [128, NDC]) for p in ("f", "b")}
        bdt_sb = {p: sb_in(pool_p2, f"bdt_sb{p}", [128, NDC]) for p in ("f", "b")}
        xdbl_sb = {p: sb_in(pool_p2, f"xdbl_{p}", [96, T]) for p in ("f", "b")}
        rw_sb = sb_in(pool_p2, "rw_sb", [128, NMC * NE])
        rb_bc = sb_in(pool_p2, "rb_bc", [128, NE])
        sel_sb = sb_in(pool_p2, "sel_sb", [96, 32 * 128], BF16)
        pool_xn = ExitStack()
        xn_bf = sb_in(pool_xn, "xn_bf", [128, NMC * T], BF16)

        pool1 = ExitStack()
        xtile = sb_in(pool1, "xtile", [128, DM])
        xm = [sb_in(pool1, f"xm{j}", [128, 256]) for j in range(NMC)]
        sq_bf = sb_in(pool1, "sq_bf", [128, 256], BF16)
        x_bf = sb_in(pool1, "x_bf", [128, 256], BF16)
        st_mean = sb_in(pool1, "st_mean", [1, 256]); st_ex2 = sb_in(pool1, "st_ex2", [1, 256])
        st_tmp = sb_in(pool1, "st_tmp", [1, 256]); st_rstd = sb_in(pool1, "st_rstd", [1, 256])
        nt1 = sb_in(pool1, "nt1", [128, 256])

        P = [psum(f"P{i}", [128, 512]) for i in range(8)]

        d_id, _ = E.dma("y", identt[:], ident)
        d_or, _ = E.dma("y", ones_rt[:], ones_r)
        d_tri, _ = E.dma("y", trit[:], tri)
        d_g, _ = E.dma("y", g_pp[:], I["norm_g"].rearrange("(c p) -> p c", p=128), slow=True)
        d_b, _ = E.dma("y", b_pp[:], I["norm_b"].rearrange("(c p) -> p c", p=128), slow=True)
        tok_ones = E.emit("v", lambda b: b.memset(onesb[:], 1.0), small=True)
        tok_onesr = E.emit("v", lambda b: b.memset(ones_rbf[:], 1.0), small=True)

        prev_ev = [None] * NMC
        prev_mm_last = None
        for tb in range(NTB):
            d_in, _ = E.dma("y", xtile[:], I["x"][tb * 128:(tb + 1) * 128, :],
                            deps=[prev_mm_last])
            mm_last = None
            for j in range(NMC):
                mm = E.emit("t", lambda b, j=j: b.matmul(
                    P[j][:, :128], xtile[:, j * 128:(j + 1) * 128], identt[:],
                    start=True, stop=True),
                    deps=[d_in, d_id, prev_ev[j]], small=True)
                ev = E.emit("v", lambda b, j=j: b.tensor_copy(xm[j][:, :128], P[j][:, :128]),
                            deps=[mm], small=True)
                prev_ev[j] = ev
                mm_last = mm
                E.dma("g", xT_h[j * 128:(j + 1) * 128, tb * 128:(tb + 1) * 128],
                      xm[j][:, :128], deps=[ev], queue="st")
            prev_mm_last = E.marker("t")
            prev_ev = [E.dmarker("st")] * NMC if False else prev_ev
            prev_ev = [E.dmarker("st") for _ in range(NMC)]
        xT_done = E.dmarker("st")

        prev_norm = None
        prev_stats = None
        for tc in range(T // 256):
            lds = []
            for j in range(NMC):
                d, _ = E.dma("y", xm[j][:, :],
                             xT_h[j * 128:(j + 1) * 128, tc * 256:(tc + 1) * 256],
                             deps=[xT_done, prev_norm])
                lds.append(d)
            alld = E.dmarker("ld")
            mm_toks = []
            sq_prev = cp_prev = None
            for j in range(NMC):
                sqt = E.emit("s", lambda b, j=j: b.activation(
                    out=sq_bf[:], in_=xm[j][:], func=AF.Square),
                    deps=[alld, mm_toks[-1] if mm_toks else None])
                cpt = E.emit("v", lambda b, j=j: b.tensor_copy(x_bf[:], xm[j][:]),
                             deps=[alld, mm_toks[-2] if len(mm_toks) >= 2 else None])
                m1 = E.emit("t", lambda b, j=j: b.matmul(
                    P[2][0:1, :256], onesb[:], x_bf[:],
                    start=(j == 0), stop=(j == NMC - 1)),
                    deps=[cpt, tok_ones, prev_stats], small=True)
                m2 = E.emit("t", lambda b, j=j: b.matmul(
                    P[3][0:1, :256], onesb[:], sq_bf[:],
                    start=(j == 0), stop=(j == NMC - 1)),
                    deps=[sqt, prev_stats], small=True)
                mm_toks += [m1, m2]
            t_mean = E.emit("v", lambda b: b.tensor_scalar(
                out=st_mean[:], in0=P[2][0:1, :256], scalar1=1.0 / DM, scalar2=None,
                op0=ALU.mult), deps=mm_toks, small=True)
            t_ex2 = E.emit("v", lambda b: b.tensor_scalar(
                out=st_ex2[:], in0=P[3][0:1, :256], scalar1=1.0 / DM, scalar2=None,
                op0=ALU.mult), deps=mm_toks, small=True)
            prev_stats = t_ex2
            t_m2 = E.emit("v", lambda b: b.tensor_tensor(
                out=st_tmp[:], in0=st_mean[:], in1=st_mean[:], op=ALU.mult),
                deps=[t_mean], small=True)
            t_var = E.emit("v", lambda b: b.tensor_tensor(
                out=st_ex2[:], in0=st_ex2[:], in1=st_tmp[:], op=ALU.subtract),
                deps=[t_ex2, t_m2], small=True)
            t_ve = E.emit("v", lambda b: b.tensor_scalar(
                out=st_ex2[:], in0=st_ex2[:], scalar1=float(EPS), scalar2=None,
                op0=ALU.add), deps=[t_var], small=True)
            t_sd = E.emit("s", lambda b: b.activation(
                out=st_tmp[:], in_=st_ex2[:], func=AF.Sqrt),
                deps=[t_ve], small=True)
            t_rs = E.emit("v", lambda b: b.reciprocal(st_rstd[:], st_tmp[:]),
                          deps=[t_sd], small=True)
            bc_m = E.emit("t", lambda b: b.matmul(
                P[4][:, :256], ones_rt[:], st_mean[:], start=True, stop=True),
                deps=[t_mean, t_m2, d_or, prev_norm], small=True)
            bc_r = E.emit("t", lambda b: b.matmul(
                P[5][:, :256], ones_rt[:], st_rstd[:], start=True, stop=True),
                deps=[t_rs, prev_norm], small=True)
            norm_toks = []
            sprev = None
            for j in range(NMC):
                s1 = E.emit("v", lambda b, j=j: b.tensor_tensor(
                    out=nt1[:], in0=xm[j][:], in1=P[4][:, :256], op=ALU.subtract),
                    deps=[bc_m, sprev])
                s2 = E.emit("v", lambda b, j=j: b.tensor_tensor(
                    out=nt1[:], in0=nt1[:], in1=P[5][:, :256], op=ALU.mult),
                    deps=[bc_r, s1])
                s3 = E.emit("v", lambda b, j=j, tc=tc: b.tensor_scalar(
                    out=xn_bf[:, j * T + tc * 256:j * T + (tc + 1) * 256],
                    in0=nt1[:], scalar1=g_pp[:, j:j + 1], scalar2=b_pp[:, j:j + 1],
                    op0=ALU.mult, op1=ALU.add),
                    deps=[d_g, d_b, s2])
                sprev = s3
                norm_toks.append(s3)
            prev_norm = norm_toks[-1]
            if debug:
                for j in range(NMC):
                    cp = E.emit("v", lambda b, j=j, tc=tc: b.tensor_copy(
                        xm[j][:], xn_bf[:, j * T + tc * 256:j * T + (tc + 1) * 256]),
                        deps=[norm_toks[j]])
                    _, et = E.dma("g", xn_h[j * 128:(j + 1) * 128, tc * 256:(tc + 1) * 256],
                                  xm[j][:], deps=[cp], queue="st")
                prev_norm = E.dmarker("st")

        pool1.close()
        E.barrier()
        SKIP_AB = stop_after == "p1"
        xn_ready = E.marker("v")
        pool_ab = ExitStack()
        wrow = sb_in(pool_ab, "wrow", [128, DM])
        wrow2 = sb_in(pool_ab, "wrow2", [128, DM])
        winuT = sb_in(pool_ab, "winuT", [128, NMC * 128], BF16)
        winzT = sb_in(pool_ab, "winzT", [128, NMC * 128], BF16)
        wx_sb = sb_in(pool_ab, "wx_sb", [96, DI])
        wxT = sb_in(pool_ab, "wxT", [128, NDC * 96], BF16)
        wdt_sb = sb_in(pool_ab, "wdt_sb", [128, RK])
        convb_sb = sb_in(pool_ab, "convb_sb", [128, NDC])
        convw_sb = sb_in(pool_ab, "convw_sb", [128, NDC * 4])
        u0_full = sb_in(pool_ab, "u0_full", [128, T])
        uc_full = sb_in(pool_ab, "uc_full", [128, T])
        ubf_d = sb_in(pool_ab, "ubf_d", [128, T], BF16)
        sz_st = sb_in(pool_ab, "sz_st", [128, T])
        u_st = u0_full

        dir_dep = None
        for p in ([] if SKIP_AB else (["f"] if stop_after == "fwd" else ["f", "b"])):
            E.barrier()
            d_wx, _ = E.dma("y", wx_sb[:], I[f"{p}_Wx"])
            d_ab, _ = E.dma("y", A_sb[p][:].rearrange("p (c n) -> p c n", n=NST),
                            I[f"{p}_Alog"].rearrange("(c p) n -> p c n", p=128), slow=True)
            d_D, _ = E.dma("y", D_sb[p][:], I[f"{p}_D"].rearrange("(c p) -> p c", p=128), slow=True)
            d_cb, _ = E.dma("y", convb_sb[:], I[f"{p}_convb"].rearrange("(c p) -> p c", p=128), slow=True)
            d_bd, _ = E.dma("y", bdt_sb[p][:], I[f"{p}_bdt"].rearrange("(c p) -> p c", p=128), slow=True)
            d_cw, _ = E.dma("y", convw_sb[:].rearrange("p (c k) -> p c k", k=4),
                            I[f"{p}_convw"].rearrange("(c p) k -> p c k", p=128), slow=True)
            tA1 = E.emit("s", lambda b, p=p: b.activation(out=A_sb[p][:], in_=A_sb[p][:], func=AF.Exp),
                         deps=[d_ab])
            tA2 = E.emit("v", lambda b, p=p: b.tensor_scalar(
                out=A_sb[p][:], in0=A_sb[p][:], scalar1=-1.0, scalar2=None, op0=ALU.mult),
                deps=[tA1])
            wx_toks = []
            for d in range(NDC):
                mm = E.emit("t", lambda b, d=d: b.matmul(
                    P[6][0:128, :96], wx_sb[:, d * 128:(d + 1) * 128],
                    identt[0:96, 0:96], start=True, stop=True),
                    deps=[d_wx, d_id, wx_toks[-1] if wx_toks else None], small=True)
                ev = E.emit("v", lambda b, d=d: b.tensor_copy(
                    wxT[:, d * 96:(d + 1) * 96], P[6][0:128, :96]), deps=[mm], small=True)
                wx_toks.append(ev)
            wdt_toks = []
            for d in range(NDC):
                d_w, _ = E.dma("y", wdt_sb[:], I[f"{p}_Wdt"][d * 128:(d + 1) * 128, :],
                               deps=[wdt_toks[-1] if wdt_toks else None])
                mm = E.emit("t", lambda b, d=d: b.matmul(
                    P[6][0:64, :128], wdt_sb[:], identt[:],
                    start=True, stop=True),
                    deps=[d_w, wdt_toks[-1] if wdt_toks else wx_toks[-1]], small=True)
                ev = E.emit("v", lambda b, d=d, p=p: b.tensor_copy(
                    wdtT[p][:, d * 128:(d + 1) * 128], P[6][0:64, :128]), deps=[mm], small=True)
                wdt_toks.append(ev)
            wprep = E.marker("v")

            prev_iter = {"c": [dir_dep, wdt_toks[-1], tA2], "x": None, "u": None,
                         "su": None, "cbf": None, "cvt": None, "e1": None, "xa": None,
                         "dsz": None, "du_st": None}
            for d in range(NDC):
                du, _ = E.dma("y", wrow[:], I[f"{p}_Win"][d * 128:(d + 1) * 128, :],
                              deps=[prev_iter["c"]])
                dz, _ = E.dma("y", wrow2[:], I[f"{p}_Win"][DI + d * 128:DI + (d + 1) * 128, :],
                              deps=[prev_iter["c"]])
                tp_toks = []
                for j in range(NMC):
                    mm = E.emit("t", lambda b, j=j: b.matmul(
                        P[6][:, :128], wrow[:, j * 128:(j + 1) * 128], identt[:],
                        start=True, stop=True),
                        deps=[du, tp_toks[-1] if tp_toks else None, prev_iter["c"]], small=True)
                    ev = E.emit("v", lambda b, j=j: b.tensor_copy(
                        winuT[:, j * 128:(j + 1) * 128], P[6][:, :128]), deps=[mm], small=True)
                    tp_toks.append(ev)
                    mm2 = E.emit("t", lambda b, j=j: b.matmul(
                        P[6][:, :128], wrow2[:, j * 128:(j + 1) * 128], identt[:],
                        start=True, stop=True), deps=[dz, ev], small=True)
                    ev2 = E.emit("v", lambda b, j=j: b.tensor_copy(
                        winzT[:, j * 128:(j + 1) * 128], P[6][:, :128]), deps=[mm2], small=True)
                    tp_toks.append(ev2)
                tpz = tp_toks[-1]
                ev_toks = []
                for tq in range(T // 512):
                    mmu = mmz = None
                    for j in range(NMC):
                        mmu = E.emit("t", lambda b, j=j, tq=tq: b.matmul(
                            P[0][:, :512], winuT[:, j * 128:(j + 1) * 128],
                            xn_bf[:, j * T + tq * 512:j * T + (tq + 1) * 512],
                            start=(j == 0), stop=(j == NMC - 1)),
                            deps=[tpz, xn_ready, ev_toks[-2] if len(ev_toks) >= 2 else None],
                            small=True)
                        mmz = E.emit("t", lambda b, j=j, tq=tq: b.matmul(
                            P[1][:, :512], winzT[:, j * 128:(j + 1) * 128],
                            xn_bf[:, j * T + tq * 512:j * T + (tq + 1) * 512],
                            start=(j == 0), stop=(j == NMC - 1)),
                            deps=[tpz, xn_ready, ev_toks[-1] if ev_toks else None],
                            small=True)
                    e1 = E.emit("s", lambda b, tq=tq: b.copy(
                        u0_full[:, tq * 512:(tq + 1) * 512], P[0][:, :512]),
                        deps=[mmu, prev_iter["cvt"], prev_iter["du_st"],
                              prev_iter["cbf"], dir_dep])
                    e2 = E.emit("s", lambda b, tq=tq: b.activation(
                        out=sz_st[:, tq * 512:(tq + 1) * 512], in_=P[1][:, :512],
                        func=AF.Silu), deps=[mmz, prev_iter["dsz"]])
                    ev_toks += [e1, e2]
                d_sz, _ = E.dma("g", sz_h[p][d * 128:(d + 1) * 128, :], sz_st[:],
                                deps=[ev_toks[-1]], queue="st")
                u0r = E.marker("s")
                cv0 = E.emit("v", lambda b, d=d: b.tensor_scalar(
                    out=uc_full[:], in0=u0_full[:],
                    scalar1=convw_sb[:, d * 4 + 3:d * 4 + 4], scalar2=None, op0=ALU.mult),
                    deps=[u0r, d_cw, prev_iter["u"]])
                cvt = cv0
                for k in (2, 1, 0):
                    s = 3 - k
                    if p == "f":
                        def cfn(b, k=k, s=s, d=d):
                            return b.scalar_tensor_tensor(
                                out=uc_full[:, s:], in0=u0_full[:, :T - s],
                                scalar=convw_sb[:, d * 4 + k:d * 4 + k + 1],
                                in1=uc_full[:, s:], op0=ALU.mult, op1=ALU.add)
                    else:
                        def cfn(b, k=k, s=s, d=d):
                            return b.scalar_tensor_tensor(
                                out=uc_full[:, :T - s], in0=u0_full[:, s:],
                                scalar=convw_sb[:, d * 4 + k:d * 4 + k + 1],
                                in1=uc_full[:, :T - s], op0=ALU.mult, op1=ALU.add)
                    cvt = E.emit("v", cfn, deps=[cvt])
                su = E.emit("s", lambda b, d=d: b.activation(
                    out=u_st[:], in_=uc_full[:], func=AF.Silu,
                    bias=convb_sb[:, d:d + 1]),
                    deps=[cvt, d_cb, prev_iter["du_st"], prev_iter["cbf"]])
                d_u, _ = E.dma("g", u_h[p][d * 128:(d + 1) * 128, :], u_st[:],
                               deps=[su], queue="st")
                cbf = E.emit("v", lambda b: b.tensor_copy(ubf_d[:], u_st[:]),
                             deps=[su, prev_iter["xa"]])
                xa_toks = []
                for tq in range(T // 512):
                    mm = E.emit("t", lambda b, d=d, tq=tq: b.matmul(
                        P[7][0:96, :512], wxT[:, d * 96:(d + 1) * 96],
                        ubf_d[:, tq * 512:(tq + 1) * 512], start=True, stop=True),
                        deps=[cbf, wprep, xa_toks[-1] if xa_toks else None], small=True)
                    if d == 0:
                        add = E.emit("v", lambda b, tq=tq, p=p: b.tensor_copy(
                            xdbl_sb[p][:, tq * 512:(tq + 1) * 512], P[7][0:96, :512]),
                            deps=[mm])
                    else:
                        add = E.emit("v", lambda b, tq=tq, p=p: b.tensor_tensor(
                            out=xdbl_sb[p][:, tq * 512:(tq + 1) * 512],
                            in0=xdbl_sb[p][:, tq * 512:(tq + 1) * 512],
                            in1=P[7][0:96, :512], op=ALU.add), deps=[mm])
                    xa_toks.append(add)
                prev_iter = {"c": xa_toks[-1], "x": None, "u": E.dmarker("st"),
                             "su": su, "cbf": cbf, "cvt": cvt, "e1": None,
                             "xa": E.marker("t"), "dsz": d_sz, "du_st": d_u}
            if debug:
                xa_fin = [E.marker("v")] + E.dmarker("st")
                for tq in range(T // 512):
                    cpd = E.emit("v", lambda b, tq=tq, p=p: b.tensor_copy(
                        u0_full[0:96, tq * 512:(tq + 1) * 512],
                        xdbl_sb[p][:, tq * 512:(tq + 1) * 512]), deps=[xa_fin])
                    E.dma("g", xdbl_h[p][0:96, tq * 512:(tq + 1) * 512],
                          u0_full[0:96, tq * 512:(tq + 1) * 512], deps=[cpd], queue="st")
            dir_dep = E.dmarker("st")

        pool_ab.close()
        pool_xn.close()
        E.barrier()

        pool_c = ExitStack()
        woutT = sb_in(pool_c, "woutT", [128, NDC * DM], BF16)
        dt_c = sb_in(pool_c, "dt_c", [128, NDC * TC])
        dtu_c = sb_in(pool_c, "dtu_c", [128, NDC * TC])
        u_c = sb_in(pool_c, "u_c", [128, NDC * TC])
        _x2alias = True
        sz_c = sb_in(pool_c, "sz_c", [128, NDC * TC])
        Bs2 = [sb_in(pool_c, f"Bs{i}", [128, TC]) for i in range(2)]
        Cs2 = [sb_in(pool_c, f"Cs{i}", [128, TC]) for i in range(2)]
        a_t2 = [sb_in(pool_c, f"a_t{i}", [128, TC]) for i in range(2)]
        b_t2 = [sb_in(pool_c, f"b_t{i}", [128, TC]) for i in range(2)]
        h_t2 = [sb_in(pool_c, f"h_t{i}", [128, TC]) for i in range(2)]
        prod2 = [sb_in(pool_c, f"prod{i}", [128, TC]) for i in range(2)]
        a_t = a_t2[0]; b_t = b_t2[0]; h_t = h_t2[0]; prod = prod2[0]
        y_acc = sb_in(pool_c, "y_acc", [128, NDC * TC])
        yp_bf = sb_in(pool_c, "yp_bf", [128, NDC * TC], BF16)
        carry = sb_in(pool_c, "carry", [128, NDC * NST])
        xdbl_bf = sb_in(pool_c, "xdbl_bf", [96, TC], BF16)
        x2_c = u_c
        xres = [sb_in(pool_c, f"xres{i}", [128, TC]) for i in range(2)]
        stg = xres
        stall = sb_in(pool_c, "stall", [128, NMC * TC])
        wrow_c = stall
        c_sqbf = sb_in(pool_c, "c_sqbf", [128, TC], BF16)
        c_xbf = sb_in(pool_c, "c_xbf", [128, TC], BF16)
        c_stm = sb_in(pool_c, "c_stm", [1, TC]); c_ste = sb_in(pool_c, "c_ste", [1, TC])
        c_stt = sb_in(pool_c, "c_stt", [1, TC]); c_str = sb_in(pool_c, "c_str", [1, TC])
        x2bf_c = yp_bf
        gt_ps = sb_in(pool_c, "gt_ps", [128, NE])
        m1_ps = sb_in(pool_c, "m1_ps", [128, 1]); m2_ps = sb_in(pool_c, "m2_ps", [128, 1])
        eq_ps = sb_in(pool_c, "eq_ps", [128, NE]); sum_ps = sb_in(pool_c, "sum_ps", [128, 1])

        d_rw, _ = E.dma("y", rw_sb[:].rearrange("p (c e) -> p c e", e=NE),
                        I["router_w"].rearrange("(c p) e -> p c e", p=128), slow=True)
        d_rb, _ = E.dma("y", rb_bc[:],
                        I["router_b"].unsqueeze(0).squeeze(0).partition_broadcast(128))
        d_sel, _ = E.dma("y", sel_sb[:], selbc)

        if stop_after in ("p1", "fwd", "ab"):
            dirs_c = []
        elif stop_after == "cfwd":
            dirs_c = ["f"]
        else:
            dirs_c = ["f", "b"]
        for p in dirs_c:
            E.barrier()
            wo_toks = []
            for mq in range(NMC):
                d_w, _ = E.dma("y", wrow_c[:], I[f"{p}_Wout"][mq * 128:(mq + 1) * 128, :],
                               deps=[wo_toks[-1] if wo_toks else None])
                for d in range(NDC):
                    mm = E.emit("t", lambda b, d=d: b.matmul(
                        P[1][:, :128], wrow_c[:, d * 128:(d + 1) * 128], identt[:],
                        start=True, stop=True),
                        deps=[d_w, wo_toks[-1] if wo_toks else None], small=True)
                    ev = E.emit("v", lambda b, d=d, mq=mq: b.tensor_copy(
                        woutT[:, d * DM + mq * 128:d * DM + (mq + 1) * 128],
                        P[1][:, :128]), deps=[mm], small=True)
                    wo_toks.append(ev)
            wo_done = wo_toks[-1]

            tc_order = range(NTC) if p == "f" else range(NTC - 1, -1, -1)
            first_tci = NTC - 1 if p == "b" else 0
            prev = {"uld": None, "dtmm": None, "soft": None, "dtu": None,
                    "yp": None, "stg": [None, None], "xr": [None, None],
                    "bc": [None, None], "x2": None, "rt": None, "wmm": None,
                    "spv": None, "sps": None}
            for tci in tc_order:
                t0 = tci * TC
                uld_toks = []
                for d in range(NDC):
                    du_, _ = E.dma("y", u_c[:, d * TC:(d + 1) * TC],
                                   u_h[p][d * 128:(d + 1) * 128, t0:t0 + TC],
                                   deps=[prev["yp"], prev["x2"], prev["rt"]])
                    ds_, _ = E.dma("y", sz_c[:, d * TC:(d + 1) * TC],
                                   sz_h[p][d * 128:(d + 1) * 128, t0:t0 + TC],
                                   deps=[prev["yp"]])
                    uld_toks += [du_, ds_]
                xc = E.emit("v", lambda b, t0=t0, p=p: b.tensor_copy(
                    xdbl_bf[:], xdbl_sb[p][:, t0:t0 + TC]), deps=[prev["dtmm"]])
                soft_toks = []
                for d in range(NDC):
                    mm = E.emit("t", lambda b, d=d, p=p: b.matmul(
                        P[0][:, :TC], wdtT[p][:, d * 128:(d + 1) * 128],
                        xdbl_bf[0:64, :], start=True, stop=True),
                        deps=[xc, prev["soft"], prev["spv"],
                              soft_toks[-1] if soft_toks else None],
                        small=True)
                    sp0 = E.emit("v", lambda b, d=d, p=p: b.tensor_scalar(
                        out=a_t[:], in0=P[0][:, :TC], scalar1=bdt_sb[p][:, d:d + 1],
                        scalar2=None, op0=ALU.add),
                        deps=[mm, prev["dtu"], prev["spv"]])
                    sp1 = E.emit("s", lambda b: b.activation(
                        out=b_t[:], in_=a_t[:], func=AF.Abs), deps=[sp0, prev["sps"]])
                    sp2 = E.emit("s", lambda b: b.activation(
                        out=b_t[:], in_=b_t[:], func=AF.Exp, scale=-1.0), deps=[sp1])
                    sp3 = E.emit("s", lambda b: b.activation(
                        out=b_t[:], in_=b_t[:], func=AF.Ln, bias=1.0), deps=[sp2])
                    sp4 = E.emit("s", lambda b: b.activation(
                        out=h_t[:], in_=a_t[:], func=AF.Relu), deps=[sp0])
                    so = E.emit("v", lambda b, d=d: b.tensor_tensor(
                        out=dt_c[:, d * TC:(d + 1) * TC], in0=b_t[:], in1=h_t[:],
                        op=ALU.add), deps=[sp3, sp4])
                    prev["spv"] = so
                    prev["sps"] = sp4
                    soft_toks.append(so)
                    tu = E.emit("v", lambda b, d=d: b.tensor_tensor(
                        out=dtu_c[:, d * TC:(d + 1) * TC],
                        in0=dt_c[:, d * TC:(d + 1) * TC],
                        in1=u_c[:, d * TC:(d + 1) * TC], op=ALU.mult),
                        deps=[so, uld_toks[2 * d]])
                prev["soft"] = soft_toks[-1]
                prev["dtmm"] = E.marker("t")
                dtu_done = [E.marker("v"), E.marker("s")]
                prev["dtu"] = dtu_done
                acc_last = None
                prevs = {}
                ac_by_d = {}
                pend = None
                kidx = 0
                for n in range(NST):
                    pb = n % 2
                    PB = P[4 + pb]
                    PC = P[6 + pb]
                    bc_dep = [E.marker("v"), E.marker("g")]
                    dB = E.emit("t", lambda b, n=n, PB=PB: b.matmul(
                        PB[:, :TC], sel_sb[:, (2 * n) * 128:(2 * n + 1) * 128],
                        xdbl_bf[:], start=True, stop=True),
                        deps=[xc, d_sel, bc_dep, prev["x2"], prev["rt"]],
                        small=True)
                    dC = E.emit("t", lambda b, n=n, PC=PC: b.matmul(
                        PC[:, :TC], sel_sb[:, (2 * n + 1) * 128:(2 * n + 2) * 128],
                        xdbl_bf[:], start=True, stop=True),
                        deps=[xc, d_sel, bc_dep, prev["x2"], prev["rt"]],
                        small=True)
                    for d in range(NDC):
                        ki = kidx % 2
                        ka = kidx % 4
                        PA = P[ka]
                        bt = b_t2[ki]; ht = h_t2[ki]; pt = prod2[ki]
                        sa = E.emit("s", lambda b, d=d, n=n, p=p, PA=PA: b.activation(
                            out=PA[:, :TC], in_=dt_c[:, d * TC:(d + 1) * TC], func=AF.Exp,
                            scale=A_sb[p][:, d * NST + n:d * NST + n + 1]),
                            deps=[dtu_done, prevs.get(("sca", ka)),
                                  prev["stg"][ka - 2] if ka >= 2 else None])
                        bt_ = E.emit("v", lambda b, d=d, PB=PB, bt=bt: b.tensor_tensor(
                            out=bt[:], in0=dtu_c[:, d * TC:(d + 1) * TC],
                            in1=PB[:, :TC], op=ALU.mult),
                            deps=[dB, dtu_done, prevs.get(("sc", ki))])
                        if pend is not None:
                            E_n, E_d, E_ki, E_sc, E_PC, E_dC = pend
                            if E_n == 0:
                                ac = E.emit("v", lambda b, E_d=E_d, E_PC=E_PC,
                                            ht2=h_t2[E_ki]: b.tensor_tensor(
                                    out=y_acc[:, E_d * TC:(E_d + 1) * TC], in0=ht2[:],
                                    in1=E_PC[:, :TC], op=ALU.mult), deps=[E_sc, E_dC])
                            else:
                                pr = E.emit("v", lambda b, E_PC=E_PC, ht2=h_t2[E_ki],
                                            pt2=prod2[E_ki]: b.tensor_tensor(
                                    out=pt2[:], in0=ht2[:], in1=E_PC[:, :TC],
                                    op=ALU.mult),
                                    deps=[E_sc, E_dC, prevs.get(("ac", E_ki))])
                                ac = E.emit("g", lambda b, E_d=E_d, pt2=prod2[E_ki]:
                                            b.tensor_tensor(
                                    out=y_acc[:, E_d * TC:(E_d + 1) * TC],
                                    in0=y_acc[:, E_d * TC:(E_d + 1) * TC], in1=pt2[:],
                                    op=ALU.add), deps=[pr])
                                prevs[("pr", E_ki)] = pr
                            prevs[("ac", E_ki)] = ac
                            ac_by_d[E_d] = ac
                            acc_last = ac
                        col = d * NST + n
                        if tci == first_tci:
                            init = 0.0
                        else:
                            init = carry[:, col:col + 1]
                        scdeps = [sa, bt_, prevs.get(("pr", ki)), prevs.get(("cr", ki))]
                        if p == "f":
                            sc_ = E.emit("v", lambda b, init=init, PA=PA, bt=bt, ht=ht:
                                         b.tensor_tensor_scan(
                                out=ht[:], data0=PA[:, :TC], data1=bt[:], initial=init,
                                op0=ALU.mult, op1=ALU.add), deps=scdeps)
                            cr = E.emit("s", lambda b, col=col, ht=ht: b.copy(
                                carry[:, col:col + 1], ht[:, TC - 1:TC]),
                                deps=[sc_], small=True)
                        else:
                            sc_ = E.emit("v", lambda b, init=init, PA=PA, bt=bt, ht=ht:
                                         b.tensor_tensor_scan(
                                out=ht[:, ::-1], data0=PA[:, :TC][:, ::-1],
                                data1=bt[:, ::-1],
                                initial=init, op0=ALU.mult, op1=ALU.add),
                                deps=scdeps)
                            cr = E.emit("s", lambda b, col=col, ht=ht: b.copy(
                                carry[:, col:col + 1], ht[:, 0:1]),
                                deps=[sc_], small=True)
                        prevs[("sc", ki)] = sc_
                        prevs[("sca", ka)] = sc_
                        prevs[("cr", ki)] = cr
                        pend = (n, d, ki, sc_, PC, dC)
                        kidx += 1
                    prev["bc"][pb] = [E.marker("v"), E.marker("g")]
                if pend is not None:
                    E_n, E_d, E_ki, E_sc, E_PC, E_dC = pend
                    if E_n == 0:
                        ac = E.emit("v", lambda b, E_d=E_d, E_PC=E_PC,
                                    ht2=h_t2[E_ki]: b.tensor_tensor(
                            out=y_acc[:, E_d * TC:(E_d + 1) * TC], in0=ht2[:],
                            in1=E_PC[:, :TC], op=ALU.mult), deps=[E_sc, E_dC])
                    else:
                        pr = E.emit("v", lambda b, E_PC=E_PC, ht2=h_t2[E_ki],
                                    pt2=prod2[E_ki]: b.tensor_tensor(
                            out=pt2[:], in0=ht2[:], in1=E_PC[:, :TC], op=ALU.mult),
                            deps=[E_sc, E_dC, prevs.get(("ac", E_ki))])
                        ac = E.emit("g", lambda b, E_d=E_d, pt2=prod2[E_ki]:
                                    b.tensor_tensor(
                            out=y_acc[:, E_d * TC:(E_d + 1) * TC],
                            in0=y_acc[:, E_d * TC:(E_d + 1) * TC], in1=pt2[:],
                            op=ALU.add), deps=[pr])
                    ac_by_d[E_d] = ac
                    acc_last = ac
                    pend = None
                yp_toks = []
                for d in range(NDC):
                    t1_ = E.emit("v", lambda b, d=d, p=p: b.tensor_scalar(
                        out=prod[:], in0=u_c[:, d * TC:(d + 1) * TC],
                        scalar1=D_sb[p][:, d:d + 1], scalar2=None, op0=ALU.mult),
                        deps=[ac_by_d[d], yp_toks[-1] if yp_toks else
                              [prevs.get(("ac", 0)), prevs.get(("ac", 1))]])
                    t2_ = E.emit("v", lambda b, d=d: b.tensor_tensor(
                        out=prod[:], in0=y_acc[:, d * TC:(d + 1) * TC], in1=prod[:],
                        op=ALU.add), deps=[t1_])
                    t3_ = E.emit("v", lambda b, d=d: b.tensor_tensor(
                        out=yp_bf[:, d * TC:(d + 1) * TC], in0=prod[:],
                        in1=sz_c[:, d * TC:(d + 1) * TC], op=ALU.mult),
                        deps=[t2_, uld_toks[2 * d + 1], prev["x2"]])
                    yp_toks.append(t3_)
                prev["yp"] = yp_toks[-1]
                if debug and yp_h[p] is not None:
                    dtap = None
                    for d in range(NDC):
                        cpy = E.emit("v", lambda b, d=d: b.tensor_copy(
                            prod[:], yp_bf[:, d * TC:(d + 1) * TC]),
                            deps=[prev["yp"], dtap])
                        dtap, _ = E.dma("g", yp_h[p][d * 128:(d + 1) * 128, t0:t0 + TC],
                                        prod[:], deps=[cpy])
                    prev["yp"] = [E.marker("v"), dtap]
                for mq in range(NMC):
                    pb = mq % 2
                    wmm = None
                    for d in range(NDC):
                        wmm = E.emit("t", lambda b, d=d, mq=mq: b.matmul(
                            P[2 + (mq % 2)][:, :TC],
                            woutT[:, d * DM + mq * 128:d * DM + (mq + 1) * 128],
                            yp_bf[:, d * TC:(d + 1) * TC],
                            start=(d == 0), stop=(d == NDC - 1)),
                            deps=[wo_done, yp_toks[d], prev["stg"][pb]], small=True)
                    if p == "f":
                        dx, _ = E.dma("y", xres[pb][:],
                                      xT_h[mq * 128:(mq + 1) * 128, t0:t0 + TC],
                                      deps=[prev["xr"][pb]])
                        sg = E.emit("v", lambda b, pb=pb, mq=mq: b.scalar_tensor_tensor(
                            out=xres[pb][:], in0=xres[pb][:], scalar=2.0,
                            in1=P[2 + pb][:, :TC], op0=ALU.mult, op1=ALU.add),
                            deps=[wmm, dx])
                        dst, _ = E.dma("g", x2pre_h[mq * 128:(mq + 1) * 128, t0:t0 + TC],
                                       xres[pb][:], deps=[sg])
                        prev["stg"][pb] = sg
                        prev["xr"][pb] = dst
                    else:
                        dx, _ = E.dma("y", xres[pb][:],
                                      x2pre_h[mq * 128:(mq + 1) * 128, t0:t0 + TC],
                                      deps=[prev["xr"][pb]])
                        sg = E.emit("v", lambda b, pb=pb, mq=mq: b.tensor_tensor(
                            out=stall[:, mq * TC:(mq + 1) * TC], in0=xres[pb][:],
                            in1=P[2 + pb][:, :TC], op=ALU.add),
                            deps=[wmm, dx, prev["x2"]])
                        prev["stg"][pb] = sg
                        prev["xr"][pb] = sg
                if p == "b":
                    stg_done = E.marker("v")
                    mm_toks2 = []
                    sq_prev2 = None
                    for mq in range(NMC):
                        sq2 = E.emit("s", lambda b, mq=mq: b.activation(
                            out=c_sqbf[:], in_=stall[:, mq * TC:(mq + 1) * TC],
                            func=AF.Square),
                            deps=[stg_done, mm_toks2[-1] if mm_toks2 else None])
                        cp2 = E.emit("v", lambda b, mq=mq: b.tensor_copy(
                            c_xbf[:], stall[:, mq * TC:(mq + 1) * TC]),
                            deps=[stg_done, mm_toks2[-2] if len(mm_toks2) >= 2 else None])
                        mA = E.emit("t", lambda b, mq=mq: b.matmul(
                            P[4][0:1, :TC], onesb[:], c_xbf[:],
                            start=(mq == 0), stop=(mq == NMC - 1)),
                            deps=[cp2, prev["x2"]], small=True)
                        mB = E.emit("t", lambda b, mq=mq: b.matmul(
                            P[5][0:1, :TC], onesb[:], c_sqbf[:],
                            start=(mq == 0), stop=(mq == NMC - 1)),
                            deps=[sq2, prev["x2"]], small=True)
                        mm_toks2 += [mA, mB]
                    tm = E.emit("v", lambda b: b.tensor_scalar(
                        out=c_stm[:], in0=P[4][0:1, :TC], scalar1=1.0 / DM,
                        scalar2=None, op0=ALU.mult), deps=mm_toks2, small=True)
                    te = E.emit("v", lambda b: b.tensor_scalar(
                        out=c_ste[:], in0=P[5][0:1, :TC], scalar1=1.0 / DM,
                        scalar2=None, op0=ALU.mult), deps=mm_toks2, small=True)
                    t2m = E.emit("v", lambda b: b.tensor_tensor(
                        out=c_stt[:], in0=c_stm[:], in1=c_stm[:], op=ALU.mult),
                        deps=[tm], small=True)
                    tv = E.emit("v", lambda b: b.tensor_tensor(
                        out=c_ste[:], in0=c_ste[:], in1=c_stt[:], op=ALU.subtract),
                        deps=[te, t2m], small=True)
                    tve = E.emit("v", lambda b: b.tensor_scalar(
                        out=c_ste[:], in0=c_ste[:], scalar1=float(EPS), scalar2=None,
                        op0=ALU.add), deps=[tv], small=True)
                    tsd = E.emit("s", lambda b: b.activation(
                        out=c_stt[:], in_=c_ste[:], func=AF.Sqrt), deps=[tve], small=True)
                    trs = E.emit("v", lambda b: b.reciprocal(c_str[:], c_stt[:]),
                                 deps=[tsd], small=True)
                    bm = E.emit("t", lambda b: b.matmul(
                        P[4][:, :TC], ones_rt[:], c_stm[:], start=True, stop=True),
                        deps=[tm, t2m, mm_toks2, prev["bc"][0]], small=True)
                    br = E.emit("t", lambda b: b.matmul(
                        P[5][:, :TC], ones_rt[:], c_str[:], start=True, stop=True),
                        deps=[trs, prev["bc"][1]], small=True)
                    x2_toks = []
                    for mq in range(NMC):
                        n1 = E.emit("v", lambda b, mq=mq: b.tensor_tensor(
                            out=x2_c[:, mq * TC:(mq + 1) * TC],
                            in0=stall[:, mq * TC:(mq + 1) * TC], in1=P[4][:, :TC],
                            op=ALU.subtract), deps=[bm])
                        n2 = E.emit("v", lambda b, mq=mq: b.tensor_tensor(
                            out=x2_c[:, mq * TC:(mq + 1) * TC],
                            in0=x2_c[:, mq * TC:(mq + 1) * TC], in1=P[5][:, :TC],
                            op=ALU.mult), deps=[br, n1])
                        n3 = E.emit("v", lambda b, mq=mq: b.tensor_copy(
                            x2bf_c[:, mq * TC:(mq + 1) * TC],
                            x2_c[:, mq * TC:(mq + 1) * TC]), deps=[n2])
                        E.dma("g", x2_h[mq * 128:(mq + 1) * 128, t0:t0 + TC],
                              x2_c[:, mq * TC:(mq + 1) * TC], deps=[n2])
                        E.dma("g", x2bf_h[mq * 128:(mq + 1) * 128, t0:t0 + TC],
                              x2bf_c[:, mq * TC:(mq + 1) * TC], deps=[n3])
                        x2_toks.append(n3)
                    rt_last = prev["rt"]
                    for tb2 in range(TC // 128):
                        tbl = (t0 + tb2 * 128) // 128
                        lg = None
                        for mq in range(NMC):
                            lg = E.emit("t", lambda b, mq=mq, tb2=tb2: b.matmul(
                                P[6][:, :NE],
                                x2_c[:, mq * TC + tb2 * 128:mq * TC + tb2 * 128 + 128],
                                rw_sb[:, mq * NE:(mq + 1) * NE],
                                start=(mq == 0), stop=(mq == NMC - 1)),
                                deps=[x2_toks[mq], d_rw, rt_last, prev["bc"][0]],
                                small=True)
                        g1 = E.emit("v", lambda b: b.tensor_tensor(
                            out=gt_ps[:], in0=P[6][:, :NE], in1=rb_bc[:], op=ALU.add),
                            deps=[lg, d_rb], small=True)
                        g2 = E.emit("s", lambda b: b.activation(
                            out=gt_ps[:], in_=gt_ps[:], func=AF.Exp), deps=[g1], small=True)
                        g3 = E.emit("v", lambda b: b.reduce_sum(
                            sum_ps[:], gt_ps[:], axis=AX.X), deps=[g2], small=True)
                        g4 = E.emit("v", lambda b: b.reciprocal(m1_ps[:], sum_ps[:]),
                                    deps=[g3], small=True)
                        g5 = E.emit("v", lambda b, tbl=tbl: b.tensor_scalar(
                            out=gates_sb[:, tbl * NE:(tbl + 1) * NE], in0=gt_ps[:],
                            scalar1=m1_ps[:], scalar2=None, op0=ALU.mult),
                            deps=[g4], small=True)
                        m1t = E.emit("v", lambda b, tbl=tbl: b.reduce_max(
                            m1_ps[:], gates_sb[:, tbl * NE:(tbl + 1) * NE], axis=AX.X),
                            deps=[g5], small=True)
                        e1t = E.emit("v", lambda b, tbl=tbl: b.tensor_scalar(
                            out=eq_ps[:], in0=gates_sb[:, tbl * NE:(tbl + 1) * NE],
                            scalar1=m1_ps[:], scalar2=None, op0=ALU.is_ge),
                            deps=[m1t], small=True)
                        e2t = E.emit("v", lambda b, tbl=tbl: b.scalar_tensor_tensor(
                            out=eq_ps[:], in0=eq_ps[:], scalar=-1e30,
                            in1=gates_sb[:, tbl * NE:(tbl + 1) * NE],
                            op0=ALU.mult, op1=ALU.add), deps=[e1t], small=True)
                        m2t = E.emit("v", lambda b: b.reduce_max(
                            m2_ps[:], eq_ps[:], axis=AX.X), deps=[e2t], small=True)
                        msk = E.emit("v", lambda b, tbl=tbl: b.tensor_scalar(
                            out=mask_sb[:, tbl * NE:(tbl + 1) * NE],
                            in0=gates_sb[:, tbl * NE:(tbl + 1) * NE],
                            scalar1=m2_ps[:], scalar2=None, op0=ALU.is_ge),
                            deps=[m2t], small=True)
                        rt_last = msk
                    prev["rt"] = rt_last
                    prev["x2"] = [E.marker("v"), E.marker("s")] + E.dmarker("st")

        pool_c.close()
        pool_p2.close()
        E.barrier()
        if stop_after in (None, "moe"):

            MTC = 512
            NMTC = T // MTC
            pool_r = ExitStack()
            runb = sb_in(pool_r, "runb", [1, NE])
            base_r = sb_in(pool_r, "base_r", [1, NE])
            cnt_r = sb_in(pool_r, "cnt_r", [8, NE])
            pm_sb = sb_in(pool_r, "pm_sb", [8, 1])
            le_r = sb_in(pool_r, "le_r", [128, NE])
            pos_sb = sb_in(pool_r, "pos_sb", [128, NE])
            val_r = sb_in(pool_r, "val_r", [128, NE])
            wT_sb = sb_in(pool_r, "wT_sb", [8, T], BF16)
            sel8_sb = sb_in(pool_r, "sel8_sb", [8, 8 * 128], BF16)

            d_pm, _ = E.dma("y", pm_sb[:], I["prefmask"])
            d_sel8, _ = E.dma("y", sel8_sb[:], sel8)
            ones32c = sb_in(pool_r, "ones32c", [128, 1])
            oz = E.emit("v", lambda b: b.memset(ones32c[:], 1.0), small=True)
            ctok = None
            for tb in range(NTB):
                ctok = E.emit("t", lambda b, tb=tb: b.matmul(
                    P[3][0:1, :NE], ones32c[:], mask_sb[:, tb * NE:(tb + 1) * NE],
                    start=(tb == 0), stop=(tb == NTB - 1)), deps=[oz, ctok], small=True)
            cnt_ev = E.emit("v", lambda b: b.tensor_copy(runb[:], P[3][0:1, :NE]),
                            deps=[ctok], small=True)
            d_ci, _ = E.dma("g", cc_in, runb[:], deps=[cnt_ev])
            if fake_cc:
                cc, _ = E.dma("g", cc_out.squeeze(0).rearrange("(c e) -> c e", e=NE)[0:1, :],
                              cc_in, deps=[d_ci])
                z8 = E.emit("v", lambda b: b.memset(cnt_r[:], 0.0), deps=[], small=True)
                d_co, _ = E.dma("g", cnt_r[0:1, :],
                                cc_out.squeeze(0).rearrange("(c e) -> c e", e=NE)[0:1, :],
                                deps=[cc, z8])
            else:
                cc = E.emit("g", lambda b: b.collective_compute(
                    "AllGather", ALU.bypass, replica_groups=[[0, 1, 2, 3, 4, 5, 6, 7]],
                    ins=[cc_in], outs=[cc_out]), deps=[d_ci], small=True)
                d_co, _ = E.dma("g", cnt_r[:],
                                cc_out.squeeze(0).rearrange("(c e) -> c e", e=NE),
                                deps=[cc])
            bmm = E.emit("t", lambda b: b.matmul(
                P[1][0:1, :NE], pm_sb[:], cnt_r[:], start=True, stop=True),
                deps=[d_co, d_pm, cnt_ev], small=True)
            bev = E.emit("v", lambda b: b.tensor_copy(base_r[:], P[1][0:1, :NE]),
                         deps=[bmm], small=True)
            rtok = E.emit("v", lambda b: b.tensor_copy(runb[:], base_r[:]),
                          deps=[bev], small=True)
            wtok = None
            c2tok = bev
            for tb in range(NTB):
                m1_ = E.emit("t", lambda b, tb=tb: b.matmul(
                    P[1][:, :NE], trit[:], mask_sb[:, tb * NE:(tb + 1) * NE],
                    start=True, stop=False), deps=[d_tri, rtok, wtok], small=True)
                m2_ = E.emit("t", lambda b: b.matmul(
                    P[1][:, :NE], ones_rt[:], runb[:], start=False, stop=True),
                    deps=[m1_], small=True)
                cpp = E.emit("v", lambda b: b.tensor_copy(pos_sb[:], P[1][:, :NE]),
                             deps=[m2_], small=True)
                c2tok = E.emit("t", lambda b, tb=tb: b.matmul(
                    P[4][0:1, :NE], ones32c[:], mask_sb[:, tb * NE:(tb + 1) * NE],
                    start=True, stop=True), deps=[oz, c2tok, cpp], small=True)
                rtok = E.emit("v", lambda b: b.tensor_tensor(
                    out=runb[:], in0=runb[:], in1=P[4][0:1, :NE], op=ALU.add),
                    deps=[c2tok], small=True)
                l1_ = E.emit("v", lambda b: b.tensor_scalar(
                    out=le_r[:], in0=pos_sb[:], scalar1=float(CAP) + 0.5, scalar2=None,
                    op0=ALU.is_le), deps=[cpp], small=True)
                v1_ = E.emit("v", lambda b, tb=tb: b.tensor_tensor(
                    out=val_r[:], in0=mask_sb[:, tb * NE:(tb + 1) * NE], in1=le_r[:],
                    op=ALU.mult), deps=[l1_], small=True)
                w1_ = E.emit("v", lambda b, tb=tb: b.tensor_tensor(
                    out=w_sb[:, tb * NE:(tb + 1) * NE], in0=val_r[:],
                    in1=gates_sb[:, tb * NE:(tb + 1) * NE], op=ALU.mult),
                    deps=[v1_], small=True)
                wt_ = E.emit("t", lambda b, tb=tb: b.matmul(
                    P[2][0:NE, :128], w_sb[:, tb * NE:(tb + 1) * NE], identt[:],
                    start=True, stop=True), deps=[w1_, wtok], small=True)
                wtok = E.emit("v", lambda b, tb=tb: b.tensor_copy(
                    wT_sb[:, tb * 128:(tb + 1) * 128], P[2][0:NE, :128]),
                    deps=[wt_], small=True)
            w_done = wtok
            if debug and rt_h is not None:
                E.dma("g", rt_h[:, 0:NTB * NE], gates_sb[:], deps=[w_done])
                E.dma("g", rt_h[:, NTB * NE:2 * NTB * NE], mask_sb[:], deps=[w_done])
                E.dma("g", rt_h[:, 2 * NTB * NE:3 * NTB * NE], w_sb[:], deps=[w_done])

            wld = [sb_in(pool_r, f"wld{i}", [128, DH // 2]) for i in range(2)]
            ym_acc = [sb_in(pool_r, f"ym_acc{i}", [128, MTC]) for i in range(2)]
            pool_m = ExitStack()
            w1bf = sb_in(pool_m, "w1bf", [128, NMC * DH], BF16)
            w2bf = sb_in(pool_m, "w2bf", [128, 32 * DM], BF16)
            x2bf_s = sb_in(pool_m, "x2bf_s", [128, NMC * MTC], BF16)
            h_bf = sb_in(pool_m, "h_bf", [128, 32 * MTC], BF16)
            b1T = sb_in(pool_m, "b1T", [128, 32 * NE])
            b2_s = sb_in(pool_m, "b2_s", [8, DM], BF16)

            for _e in range(NE):
                d_b1, _ = E.dma("y",
                                b1T[:].rearrange("p (h e) -> p h e", h=32)[:, :, _e],
                                I["b1"][_e].rearrange("(h p) -> p h", p=128),
                                slow=True)
            d_b2, _ = E.dma("g", b2_s[:], I["b2"])
            ib_toks = {}
            prev_ib_ev = None
            for tci in range(NMTC):
                t0 = tci * MTC
                for mq in range(NMC):
                    ib = E.emit("t", lambda b, mq=mq, t0=t0: b.matmul(
                        P[3][:, :MTC], b2_s[:, mq * 128:(mq + 1) * 128], wT_sb[:, t0:t0 + MTC],
                        start=True, stop=True), deps=[d_b2, w_done, prev_ib_ev],
                        small=True)
                    ya = ym_acc[(tci * NMC + mq) % 2]
                    ev_ = E.emit("v", lambda b, ya=ya: b.tensor_copy(
                        ya[:], P[3][:, :MTC]),
                        deps=[ib, ib_toks.get((tci * NMC + mq - 2, "roll"))])
                    dst, _ = E.dma("g", ymoe_h[mq * 128:(mq + 1) * 128, t0:t0 + MTC],
                                   ya[:], deps=[ev_])
                    ib_toks[(tci, mq)] = dst
                    ib_toks[(tci * NMC + mq, "roll")] = dst
                    prev_ib_ev = ev_
            prev_ymm = [E.marker("v"), E.marker("v")]
            mm_h = None
            prev_wb = None
            for e in range(NE):
                lasth = mm_h if e else None
                w1toks = []
                for mq in range(NMC):
                    dw, _ = E.dma("g", w1bf[:, mq * DH:(mq + 1) * DH],
                                  I["w1"][e, mq * 128:(mq + 1) * 128, :],
                                  deps=[lasth])
                    w1toks.append(dw)
                w2toks = []
                for hq in range(32):
                    dw, _ = E.dma("g", w2bf[:, hq * DM:(hq + 1) * DM],
                                  I["w2"][e, hq * 128:(hq + 1) * 128, :],
                                  deps=[prev_ymm[0], prev_ymm[1]])
                    w2toks.append(dw)
                wcast = [w1toks, w2toks]
                for tci in range(NMTC):
                    t0 = tci * MTC
                    x2l = []
                    for mq in range(NMC):
                        dx2, _ = E.dma("y", x2bf_s[:, mq * MTC:(mq + 1) * MTC],
                                       x2bf_h[mq * 128:(mq + 1) * 128, t0:t0 + MTC],
                                       deps=[mm_h])
                        x2l.append(dx2)
                    wb = E.emit("t", lambda b, e=e, t0=t0: b.matmul(
                        P[4][:, :MTC], sel8_sb[:, e * 128:(e + 1) * 128], wT_sb[:, t0:t0 + MTC],
                        start=True, stop=True), deps=[w_done, d_sel8, prev_wb], small=True)
                    wbe = wb
                    gel_toks = []
                    for hq in range(32):
                        pb2 = hq % 2
                        mm_h = None
                        for mq in range(NMC):
                            mm_h = E.emit("t", lambda b, hq=hq, mq=mq: b.matmul(
                                P[0 + (hq % 2)][:, :MTC],
                                w1bf[:, mq * DH + hq * 128:mq * DH + (hq + 1) * 128],
                                x2bf_s[:, mq * MTC:(mq + 1) * MTC],
                                start=(mq == 0), stop=(mq == NMC - 1)),
                                deps=[wcast, x2l[mq],
                                      gel_toks[-2] if len(gel_toks) >= 2 else None],
                                small=True)
                        ge = E.emit("s", lambda b, hq=hq, e=e: b.activation(
                            out=h_bf[:, hq * MTC:(hq + 1) * MTC], in_=P[0 + (hq % 2)][:, :MTC],
                            func=AF.Gelu, bias=b1T[:, hq * NE + e:hq * NE + e + 1]),
                            deps=[mm_h], small=True)
                        gel_toks.append(ge)
                    for mq in range(NMC):
                        mm_y = None
                        for hq in range(32):
                            mm_y = E.emit("t", lambda b, hq=hq, mq=mq: b.matmul(
                                P[2 + (mq % 2)][:, :MTC],
                                w2bf[:, hq * DM + mq * 128:hq * DM + (mq + 1) * 128],
                                h_bf[:, hq * MTC:(hq + 1) * MTC],
                                start=(hq == 0), stop=(hq == 31)),
                                deps=[gel_toks[hq], prev_ymm[mq % 2]], small=True)
                        ya = ym_acc[(tci * NMC + mq) % 2]
                        cy = E.emit("s", lambda b, ya=ya, mq=mq: b.copy(
                            ya[:], P[2 + (mq % 2)][:, :MTC]),
                            deps=[mm_y, ib_toks.get(((tci * NMC + mq - 2) % (NMTC * NMC), "roll"))])
                        ct = E.emit("v", lambda b, ya=ya: b.tensor_tensor(
                            out=ya[:], in0=ya[:], in1=P[4][:, :MTC], op=ALU.mult),
                            deps=[cy, wbe])
                        prev_ymm[mq % 2] = ct
                        prev_wb = ct
                        dacc, _ = E.dma("g", ymoe_h[mq * 128:(mq + 1) * 128, t0:t0 + MTC],
                                        ya[:], deps=[ct, ib_toks[(tci, mq)]], accum=True)
                        ib_toks[(tci, mq)] = dacc
                        ib_toks[(tci * NMC + mq, "roll")] = dacc

            pool_m.close()
            E.barrier()
            f_x2 = sb_in(pool_r, "f_x2", [128, NMC * MTC])
            f_stg = sb_in(pool_r, "f_stg", [128, DM])
            f_sq = sb_in(pool_r, "f_sq", [128, MTC], BF16)
            f_xb = sb_in(pool_r, "f_xb", [128, MTC], BF16)
            f_m = sb_in(pool_r, "f_m", [1, MTC]); f_e2 = sb_in(pool_r, "f_e2", [1, MTC])
            f_t = sb_in(pool_r, "f_t", [1, MTC]); f_r = sb_in(pool_r, "f_r", [1, MTC])
            prev_f = {"ld": None, "mm": None, "st": None, "tp": None}
            for tci in range(NMTC):
                t0 = tci * MTC
                ld_toks = []
                for mq in range(NMC):
                    wl = wld[mq % 2]
                    d1, _ = E.dma("y", f_x2[:, mq * MTC:(mq + 1) * MTC],
                                  x2_h[mq * 128:(mq + 1) * 128, t0:t0 + MTC],
                                  deps=[prev_f["mm"]])
                    d2, _ = E.dma("y", wl[:, :MTC],
                                  ymoe_h[mq * 128:(mq + 1) * 128, t0:t0 + MTC],
                                  deps=[prev_f["mm"],
                                        ld_toks[mq - 2] if mq >= 2 else None])
                    ad = E.emit("v", lambda b, mq=mq, wl=wl: b.tensor_tensor(
                        out=f_x2[:, mq * MTC:(mq + 1) * MTC],
                        in0=f_x2[:, mq * MTC:(mq + 1) * MTC],
                        in1=wl[:, :MTC], op=ALU.add),
                        deps=[d1, d2])
                    ld_toks.append(ad)
                mmt = []
                for mq in range(NMC):
                    sq2 = E.emit("s", lambda b, mq=mq: b.activation(
                        out=f_sq[:], in_=f_x2[:, mq * MTC:(mq + 1) * MTC], func=AF.Square),
                        deps=[ld_toks[mq], mmt[-1] if mmt else None])
                    cp2 = E.emit("v", lambda b, mq=mq: b.tensor_copy(
                        f_xb[:], f_x2[:, mq * MTC:(mq + 1) * MTC]),
                        deps=[ld_toks[mq], mmt[-2] if len(mmt) >= 2 else None])
                    mA = E.emit("t", lambda b, mq=mq: b.matmul(
                        P[5][0:1, :MTC], onesb[:], f_xb[:],
                        start=(mq == 0), stop=(mq == NMC - 1)),
                        deps=[cp2, prev_f["mm"]], small=True)
                    mB = E.emit("t", lambda b, mq=mq: b.matmul(
                        P[6][0:1, :MTC], onesb[:], f_sq[:],
                        start=(mq == 0), stop=(mq == NMC - 1)),
                        deps=[sq2, prev_f["mm"]], small=True)
                    mmt += [mA, mB]
                tm = E.emit("v", lambda b: b.tensor_scalar(
                    out=f_m[:], in0=P[5][0:1, :MTC], scalar1=1.0 / DM, scalar2=None,
                    op0=ALU.mult), deps=mmt, small=True)
                te = E.emit("v", lambda b: b.tensor_scalar(
                    out=f_e2[:], in0=P[6][0:1, :MTC], scalar1=1.0 / DM, scalar2=None,
                    op0=ALU.mult), deps=mmt, small=True)
                t2m = E.emit("v", lambda b: b.tensor_tensor(
                    out=f_t[:], in0=f_m[:], in1=f_m[:], op=ALU.mult), deps=[tm], small=True)
                tv = E.emit("v", lambda b: b.tensor_tensor(
                    out=f_e2[:], in0=f_e2[:], in1=f_t[:], op=ALU.subtract),
                    deps=[te, t2m], small=True)
                tve = E.emit("v", lambda b: b.tensor_scalar(
                    out=f_e2[:], in0=f_e2[:], scalar1=float(EPS), scalar2=None, op0=ALU.add),
                    deps=[tv], small=True)
                tsd = E.emit("s", lambda b: b.activation(
                    out=f_t[:], in_=f_e2[:], func=AF.Sqrt), deps=[tve], small=True)
                trs = E.emit("v", lambda b: b.reciprocal(f_r[:], f_t[:]), deps=[tsd], small=True)
                bm = E.emit("t", lambda b: b.matmul(
                    P[5][:, :MTC], ones_rt[:], f_m[:], start=True, stop=True),
                    deps=[tm, t2m, mmt, prev_f["tp"]], small=True)
                br = E.emit("t", lambda b: b.matmul(
                    P[6][:, :MTC], ones_rt[:], f_r[:], start=True, stop=True),
                    deps=[trs, prev_f["tp"]], small=True)
                x3toks = []
                for mq in range(NMC):
                    n1 = E.emit("v", lambda b, mq=mq: b.tensor_tensor(
                        out=f_x2[:, mq * MTC:(mq + 1) * MTC],
                        in0=f_x2[:, mq * MTC:(mq + 1) * MTC], in1=P[5][:, :MTC],
                        op=ALU.subtract), deps=[bm])
                    n2 = E.emit("v", lambda b, mq=mq: b.tensor_tensor(
                        out=f_x2[:, mq * MTC:(mq + 1) * MTC],
                        in0=f_x2[:, mq * MTC:(mq + 1) * MTC], in1=P[6][:, :MTC],
                        op=ALU.mult), deps=[br, n1])
                    x3toks.append(n2)
                for tb2 in range(MTC // 128):
                    tp_toks = []
                    for mq in range(NMC):
                        tp = E.emit("t", lambda b, mq=mq, tb2=tb2: b.matmul(
                            P[7][:, :128],
                            f_x2[:, mq * MTC + tb2 * 128:mq * MTC + tb2 * 128 + 128],
                            identt[:], start=True, stop=True),
                            deps=[x3toks[mq], prev_f["st"], tp_toks[-1] if tp_toks else None],
                            small=True)
                        ev2 = E.emit("v", lambda b, mq=mq: b.tensor_copy(
                            f_stg[:, mq * 128:(mq + 1) * 128], P[7][:, :128]),
                            deps=[tp], small=True)
                        tp_toks.append(ev2)
                    dout, _ = E.dma("g", out_h[t0 + tb2 * 128:t0 + tb2 * 128 + 128, :],
                                    f_stg[:], deps=[tp_toks[-1]])
                    prev_f["st"] = dout
                prev_f["tp"] = E.marker("t")
                prev_f["mm"] = [E.marker("v"), E.marker("s"), prev_f["st"]]
            pool_r.close()


        E.barrier()
        pool_tail = ExitStack()
        if stop_after is not None:
            ztile = sb_in(pool_tail, "ztile", [128, 256])
            z = E.emit("v", lambda b: b.memset(ztile[:, :], 0.0),
                       deps=[E.dmarker("all")])
            for tb in range(NTB):
                for j in range(4):
                    E.dma("g", out_h[tb * 128:(tb + 1) * 128, j * 256:(j + 1) * 256],
                          ztile[:, :], deps=[z], queue="st")
        fin = E.dmarker("all")
        E.emit("g", lambda b: b.engine_nop(), deps=[fin], small=True)
        pool_tail.close()

        E.finalize(block, sems, dsems)

    return nc


def make_in_maps(inputs):
    ren = {}
    for k, v in inputs.items():
        if k == "L0_attn_x":
            ren["x_full"] = np.ascontiguousarray(np.asarray(v))
        elif k.startswith("L0_attn_"):
            ren[k[len("L0_attn_"):]] = np.ascontiguousarray(np.asarray(v))
        elif k.startswith("L0_ffn_"):
            ren[k[len("L0_ffn_"):]] = np.ascontiguousarray(np.asarray(v))
    in_maps = []
    for c in range(8):
        m = {"x": ren["x_full"][c]}
        for p in ("f", "b"):
            for nm in ("Win", "convb", "Wx", "Wdt", "bdt", "Alog", "D", "Wout"):
                m[f"{p}_{nm}"] = ren[f"{p}_{nm}"]
            m[f"{p}_convw"] = np.ascontiguousarray(ren[f"{p}_convw"].reshape(DI, 4))
        for nm in ("router_w", "router_b", "w1", "b1", "w2", "b2"):
            m[nm] = ren[nm]
        m["norm_g"] = ren["norm_g"]
        m["norm_b"] = ren["norm_b"]
        m["prefmask"] = (np.arange(NE) < c).astype(np.float32).reshape(NE, 1)
        in_maps.append(m)
    return in_maps


def kernel(**inputs):
    nc = build_nc(debug=False)
    in_maps = make_in_maps(inputs)
    res = run_bass_kernel_spmd(nc, in_maps, core_ids=list(range(8)))
    return np.stack([r["out"] for r in res.results])
